# version 08
# baseline (speedup 1.0000x reference)
import numpy as np
from contextlib import ExitStack
import concourse.bass as bass
import concourse.mybir as mybir
from concourse.bass_utils import run_bass_kernel_spmd

F32 = mybir.dt.float32
BF16 = mybir.dt.bfloat16
I32 = mybir.dt.int32
AF = mybir.ActivationFunctionType
ALU = mybir.AluOpType

B = 2
L = 2048
D = 1024
DI = 2048
DS = 16
DTR = 64
T1 = 256
T2 = 128
NC1 = L // T1
NC2 = L // T2
EPS = 1e-5
R1024 = 1.0 / 1024.0


def build_nc():
    nc = bass.Bass("TRN2", target_bir_lowering=False, detect_race_conditions=False)

    x_in = nc.dram_tensor("x_in", [B, L, D], F32, kind="ExternalInput").ap()
    w_in = nc.dram_tensor("w_in", [D, 2 * DI], F32, kind="ExternalInput").ap()
    wxp_in = nc.dram_tensor("wxp_in", [DI, DTR + 2 * DS], F32, kind="ExternalInput").ap()
    wdt_in = nc.dram_tensor("wdt_in", [DTR, DI], F32, kind="ExternalInput").ap()
    wout_in = nc.dram_tensor("wout_in", [DI, D], F32, kind="ExternalInput").ap()
    convw_in = nc.dram_tensor("convw_in", [128, 64], F32, kind="ExternalInput").ap()
    convb_in = nc.dram_tensor("convb_in", [128, 16], F32, kind="ExternalInput").ap()
    bdt_in = nc.dram_tensor("bdt_in", [128, 16], F32, kind="ExternalInput").ap()
    alog_in = nc.dram_tensor("alog_in", [128, 256], F32, kind="ExternalInput").ap()
    dD_in = nc.dram_tensor("dD_in", [128, 16], F32, kind="ExternalInput").ap()
    qT_in = nc.dram_tensor("qT_in", [6, D, D], I32, kind="ExternalInput").ap()
    scT_in = nc.dram_tensor("scT_in", [6, 64, D], F32, kind="ExternalInput").ap()
    bias_in = nc.dram_tensor("bias_in", [128, 48], F32, kind="ExternalInput").ap()
    laT_in = nc.dram_tensor("laT_in", [6, D, 32], F32, kind="ExternalInput").ap()
    lbT_in = nc.dram_tensor("lbT_in", [6, 32, D], F32, kind="ExternalInput").ap()
    gam_in = nc.dram_tensor("gam_in", [128, 24], F32, kind="ExternalInput").ap()
    bet_in = nc.dram_tensor("bet_in", [128, 24], F32, kind="ExternalInput").ap()
    id_in = nc.dram_tensor("id_in", [128, 128], F32, kind="ExternalInput").ap()
    ones16_in = nc.dram_tensor("ones16_in", [16, 128], F32, kind="ExternalInput").ap()
    ones1_in = nc.dram_tensor("ones1_in", [1, 128], F32, kind="ExternalInput").ap()
    onescol_in = nc.dram_tensor("onescol_in", [128, 1], F32, kind="ExternalInput").ap()
    e64_in = nc.dram_tensor("e64_in", [128, 1024], F32, kind="ExternalInput").ap()

    out_hbm = nc.dram_tensor("out", [B, L, D], F32, kind="ExternalOutput").ap()
    g_scr = nc.dram_tensor("g_scr", [B, L, D], BF16, kind="ExternalOutput").ap()
    sz_scr = nc.dram_tensor("sz_scr", [B, NC2, 128, DI], BF16, kind="ExternalOutput").ap()
    bc_scr = nc.dram_tensor("bc_scr", [32, 128], BF16, kind="Internal").ap()

    es = ExitStack()
    EN = es.enter_context
    with es:
        LD = EN(nc.semaphore("LD"))
        ST = EN(nc.semaphore("ST"))
        TE = EN(nc.semaphore("TE"))
        VE = EN(nc.semaphore("VE"))
        SE = EN(nc.semaphore("SE"))
        cnt = {"LD": 0, "ST": 0, "TE": 0, "VE": 0, "SE": 0}
        SEM = {"LD": LD, "ST": ST, "TE": TE, "VE": VE, "SE": SE}
        pfree = {}

        def inc(instr, s, k=1):
            cnt[s] += k
            instr.then_inc(SEM[s], k)

        def wait_psum(eng, *names):
            for n in names:
                for s, v in pfree.get(n, {}).items():
                    eng.wait_ge(SEM[s], v)

        def free_psum(name, *sems):
            d = pfree.setdefault(name, {})
            for s in sems:
                d[s] = cnt[s]

        def vdrain(vector):
            vector.wait_ge(VE, cnt["VE"])

        def sdrain(scalar):
            scalar.wait_ge(SE, cnt["SE"])

        x_tm = EN(nc.sbuf_tensor("x_tm", [128, 2048], F32))
        xn_tm = EN(nc.sbuf_tensor("xn_tm", [128, 2048], BF16))
        sq = EN(nc.sbuf_tensor("sq", [128, 2048], F32))
        mu_sb = EN(nc.sbuf_tensor("mu_sb", [128, 2], F32))
        m2_sb = EN(nc.sbuf_tensor("m2_sb", [128, 2], F32))
        rs_sb = EN(nc.sbuf_tensor("rs_sb", [128, 2], F32))
        mu2_sb = EN(nc.sbuf_tensor("mu2_sb", [128, 2], F32))
        xnT_m = EN(nc.sbuf_tensor("xnT_m", [128, 8, 128], BF16))
        out_tm = EN(nc.sbuf_tensor("out_tm", [128, 1024], F32))
        cmb = EN(nc.sbuf_tensor("cmb", [128, 1024], F32))
        gtm = EN(nc.sbuf_tensor("gtm", [128, 2048], BF16))
        id_bf = EN(nc.sbuf_tensor("id_bf", [128, 128], BF16))
        ones16_bf = EN(nc.sbuf_tensor("ones16_bf", [16, 128], BF16))
        ones1_bf = EN(nc.sbuf_tensor("ones1_bf", [1, 128], BF16))
        onescol_bf = EN(nc.sbuf_tensor("onescol_bf", [128, 1], BF16))
        e64_bf = EN(nc.sbuf_tensor("e64_bf", [128, 1024], BF16))
        cst = EN(nc.sbuf_tensor("cst", [128, 512], F32))
        acol = EN(nc.sbuf_tensor("acol", [128, 256], F32))
        mrs = EN(nc.sbuf_tensor("mrs", [1, 768], F32))
        mrs_bf = EN(nc.sbuf_tensor("mrs_bf", [1, 512], BF16))

        p_mm = EN(nc.psum_tensor("p_mm", [128, 512], F32))
        p_mm2 = EN(nc.psum_tensor("p_mm2", [128, 512], F32))
        p_rep = EN(nc.psum_tensor("p_rep", [128, 512], F32))
        p_ln = EN(nc.psum_tensor("p_ln", [128, 512], F32))
        p_sm = EN(nc.psum_tensor("p_sm", [128, 512], F32))
        p_tp = EN(nc.psum_tensor("p_tp", [128, 1024], BF16))
        p_tp2 = EN(nc.psum_tensor("p_tp2", [128, 1024], BF16))

        block = EN(nc.Block())

        @block.sync
        def _(sync):
            for src, dst in [(convw_in, cst[:, 0:64]), (convb_in, cst[:, 64:80]),
                             (bdt_in, cst[:, 80:96]), (dD_in, cst[:, 96:112]),
                             (bias_in, cst[:, 112:160]), (gam_in, cst[:, 160:184]),
                             (bet_in, cst[:, 184:208]), (alog_in, cst[:, 208:464]),
                             (id_in, sq[:, 0:128]), (ones16_in, sq[0:16, 128:256]),
                             (ones1_in, sq[0:1, 256:384]), (onescol_in, sq[:, 384:385]),
                             (e64_in, x_tm[:, 0:1024])]:
                inc(sync.dma_start(out=dst, in_=src), "LD", 16)
        ld0 = cnt["LD"]

        @block.scalar
        def _(scalar):
            scalar.wait_ge(LD, ld0)
            inc(scalar.activation(out=acol[:], in_=cst[:, 208:464], func=AF.Exp), "SE")
        se0 = cnt["SE"]

        @block.vector
        def _(vector):
            vector.wait_ge(LD, ld0)
            vector.tensor_copy(out=id_bf[:], in_=sq[:, 0:128])
            vector.tensor_copy(out=ones16_bf[:], in_=sq[0:16, 128:256])
            vector.tensor_copy(out=ones1_bf[:], in_=sq[0:1, 256:384])
            vector.tensor_copy(out=onescol_bf[:], in_=sq[:, 384:385])
            vector.tensor_copy(out=e64_bf[:], in_=x_tm[:, 0:1024])
            vector.memset(cst[:, 464:465], EPS)
            vector.wait_ge(SE, se0)
            inc(vector.tensor_scalar(out=acol[:], in0=acol[:], scalar1=-1.0,
                                     scalar2=None, op0=ALU.mult), "VE")

        hist = {-1: dict(cnt)}

        def chunk_gate(sync, h):
            sync.wait_ge(TE, h["TE"])
            sync.wait_ge(VE, h["VE"])
            sync.wait_ge(SE, h["SE"])
            sync.wait_ge(ST, h["ST"])

        def emit_ln(ld_need, ntt, ve_extra=None):
            @block.vector
            def _(vector):
                vector.wait_ge(LD, ld_need)
                if ve_extra is not None:
                    vector.wait_ge(VE, ve_extra)
                inc(vector.reduce_sum(
                    out=mu_sb[:, 0:ntt],
                    in_=x_tm[:, 0:ntt * 1024].rearrange("p (a d) -> p a d", a=ntt),
                    axis=mybir.AxisListType.X), "VE")
                vdrain(vector)
                inc(vector.tensor_scalar(out=mu_sb[:, 0:ntt], in0=mu_sb[:, 0:ntt],
                                         scalar1=R1024, scalar2=None, op0=ALU.mult), "VE")
                vdrain(vector)
            v_mu = cnt["VE"]

            @block.scalar
            def _(scalar):
                scalar.wait_ge(LD, ld_need)
                inc(scalar.activation(out=sq[:, 0:ntt * 1024], in_=x_tm[:, 0:ntt * 1024],
                                      func=AF.Square), "SE")
            s_sq = cnt["SE"]

            @block.vector
            def _(vector):
                vector.wait_ge(SE, s_sq)
                inc(vector.reduce_sum(
                    out=m2_sb[:, 0:ntt],
                    in_=sq[:, 0:ntt * 1024].rearrange("p (a d) -> p a d", a=ntt),
                    axis=mybir.AxisListType.X), "VE")
                vdrain(vector)
                inc(vector.tensor_scalar(out=m2_sb[:, 0:ntt], in0=m2_sb[:, 0:ntt],
                                         scalar1=R1024, scalar2=None, op0=ALU.mult), "VE")
                vdrain(vector)
                inc(vector.tensor_tensor(out=mu2_sb[:, 0:ntt], in0=mu_sb[:, 0:ntt],
                                         in1=mu_sb[:, 0:ntt], op=ALU.mult), "VE")
                vdrain(vector)
                inc(vector.tensor_tensor(out=rs_sb[:, 0:ntt], in0=m2_sb[:, 0:ntt],
                                         in1=mu2_sb[:, 0:ntt], op=ALU.subtract), "VE")
                vdrain(vector)
            v_var = cnt["VE"]

            @block.scalar
            def _(scalar):
                scalar.wait_ge(VE, v_var)
                inc(scalar.activation(out=rs_sb[:, 0:ntt], in_=rs_sb[:, 0:ntt],
                                      func=AF.Sqrt, bias=cst[:, 464:465]), "SE")
            s_rs = cnt["SE"]

            @block.vector
            def _(vector):
                vector.wait_ge(SE, s_rs)
                inc(vector.reciprocal(out=rs_sb[:, 0:ntt], in_=rs_sb[:, 0:ntt]), "VE")
                vdrain(vector)
                for a in range(ntt):
                    inc(vector.tensor_scalar(
                        out=xn_tm[:, a * 1024:(a + 1) * 1024],
                        in0=x_tm[:, a * 1024:(a + 1) * 1024],
                        scalar1=mu_sb[:, a:a + 1], scalar2=rs_sb[:, a:a + 1],
                        op0=ALU.subtract, op1=ALU.mult), "VE")
            return cnt["VE"]

        def emit_tp(src_slices, dst_psum, dst_name, wait_ve):
            @block.tensor
            def _(tensor):
                tensor.wait_ge(VE, wait_ve)
                wait_psum(tensor, dst_name)
                for d, srcap in enumerate(src_slices):
                    inc(tensor.transpose(out=dst_psum[:, d * 128:(d + 1) * 128],
                                         in_=srcap, identity=id_bf[:]), "TE")
            return cnt["TE"]

        s1 = ExitStack()
        E1 = s1.enter_context
        w_deq = E1(nc.sbuf_tensor("w_deq", [128, 6, 8, 1024], BF16))
        la_sb = E1(nc.sbuf_tensor("la_sb", [128, 6, 256], BF16))
        lb_sb = E1(nc.sbuf_tensor("lb_sb", [128, 2, 1024], BF16))
        qbuf = E1(nc.sbuf_tensor("qbuf", [128, 1024], I32))
        qf = E1(nc.sbuf_tensor("qf", [128, 2048], F32))
        scT_sb = E1(nc.sbuf_tensor("scT_sb", [128, 3, 1024], BF16))
        xnT_f = E1(nc.sbuf_tensor("xnT_f", [128, 8, 256], BF16))
        h0 = E1(nc.sbuf_tensor("h0", [128, 8, 256], F32))
        h1 = E1(nc.sbuf_tensor("h1", [128, 8, 256], F32))
        hbf = E1(nc.sbuf_tensor("hbf", [128, 8, 256], BF16))
        yv = E1(nc.sbuf_tensor("yv", [128, 8, 256], F32))
        ybf = E1(nc.sbuf_tensor("ybf", [128, 8, 256], BF16))
        y2bf = E1(nc.sbuf_tensor("y2bf", [128, 8, 256], BF16))
        rbf = E1(nc.sbuf_tensor("rbf", [128, 256], BF16))
        gT_bf = hbf

        for j in range(6):
            @block.sync
            def _(sync, j=j):
                sync.wait_ge(VE, cnt["VE"])
                inc(sync.dma_start(out=x_tm[0:64, 0:1024], in_=scT_in[j]), "LD", 16)
                inc(sync.dma_start(
                    out=qf[:, 1024:1280].rearrange("p (t r) -> p t r", t=8),
                    in_=laT_in[j].rearrange("(t p) r -> p t r", p=128)), "LD", 16)
                inc(sync.dma_start(out=x_tm[0:32, 1024:2048], in_=lbT_in[j]), "LD", 16)
            ldj = cnt["LD"]

            @block.vector
            def _(vector, j=j):
                vector.wait_ge(LD, ldj)
                inc(vector.tensor_copy(out=scT_sb[64 * (j % 2):64 * (j % 2) + 64, j // 2, :],
                                       in_=x_tm[0:64, 0:1024]), "VE")
                inc(vector.tensor_copy(out=la_sb[:, j, :], in_=qf[:, 1024:1280]), "VE")
                inc(vector.tensor_copy(out=lb_sb[32 * (j % 3):32 * (j % 3) + 32, j // 3, :],
                                       in_=x_tm[0:32, 1024:2048]), "VE")

        def sc_slice(j, c0, c1):
            return scT_sb[64 * (j % 2):64 * (j % 2) + 64, j // 2, c0:c1]

        def lb_slice(j, c0, c1):
            return lb_sb[32 * (j % 3):32 * (j % 3) + 32, j // 3, c0:c1]

        for j in range(6):
            for t in range(8):
                @block.sync
                def _(sync):
                    sync.wait_ge(VE, cnt["VE"])
                    sync.wait_ge(SE, cnt["SE"])
                    inc(sync.dma_start(out=qbuf[:], in_=qT_in[j, 128 * t:128 * (t + 1), :]),
                        "LD", 16)
                ldq = cnt["LD"]

                @block.vector
                def _(vector):
                    vector.wait_ge(LD, ldq)
                    inc(vector.tensor_copy(out=qf[:, 0:1024], in_=qbuf[:]), "VE")
                vq = cnt["VE"]

                @block.scalar
                def _(scalar):
                    scalar.wait_ge(VE, vq)
                    inc(scalar.activation(out=qf[:, 0:1024], in_=qf[:, 0:1024], func=AF.Copy,
                                          scale=1.0 / 7.5, bias=-1.0), "SE")
                sqd = cnt["SE"]

                @block.tensor
                def _(tensor, j=j, t=t):
                    tensor.wait_ge(SE, sqd)
                    wait_psum(tensor, "p_rep", "p_mm")
                    p0 = 64 * (j % 2)
                    inc(tensor.matmul(out=p_rep[:, 0:512],
                                      lhsT=e64_bf[p0:p0 + 64, 128 * t:128 * (t + 1)],
                                      rhs=sc_slice(j, 0, 512),
                                      start=True, stop=True), "TE")
                    inc(tensor.matmul(out=p_mm[:, 0:512],
                                      lhsT=e64_bf[p0:p0 + 64, 128 * t:128 * (t + 1)],
                                      rhs=sc_slice(j, 512, 1024),
                                      start=True, stop=True), "TE")
                ter = cnt["TE"]

                @block.vector
                def _(vector, j=j, t=t):
                    vector.wait_ge(TE, ter)
                    inc(vector.tensor_tensor(out=w_deq[:, j, t, 0:512],
                                             in0=qf[:, 0:512],
                                             in1=p_rep[:, 0:512], op=ALU.mult), "VE")
                    inc(vector.tensor_tensor(out=w_deq[:, j, t, 512:1024],
                                             in0=qf[:, 512:1024],
                                             in1=p_mm[:, 0:512], op=ALU.mult), "VE")
                free_psum("p_rep", "VE")
                free_psum("p_mm", "VE")

        hist[("ffn", -1)] = dict(cnt)

        def ffn_half(iw, rhs_bf, rhs_ready_sem, rhs_ready):
            @block.tensor
            def _(tensor):
                tensor.wait_ge(SEM[rhs_ready_sem], rhs_ready)
                wait_psum(tensor, "p_sm")
                for k in range(8):
                    inc(tensor.matmul(out=p_sm[0:32, 0:256],
                                      lhsT=la_sb[:, iw, :].rearrange("p (t r) -> p t r", t=8)[:, k, :],
                                      rhs=rhs_bf[:, k, :],
                                      start=(k == 0), stop=(k == 7)), "TE")
            te_r = cnt["TE"]

            rb0 = 32 * (iw % 3)

            @block.vector
            def _(vector):
                vector.wait_ge(TE, te_r)
                inc(vector.tensor_copy(out=rbf[rb0:rb0 + 32, :], in_=p_sm[0:32, 0:256]), "VE")
            free_psum("p_sm", "VE")
            v_r = cnt["VE"]

            for m in range(8):
                slot = "p_mm" if m % 2 == 0 else "p_mm2"
                pb = p_mm if slot == "p_mm" else p_mm2
                co = 0
                key = slot

                @block.tensor
                def _(tensor, m=m, pb=pb, co=co, key=key):
                    if m == 0:
                        tensor.wait_ge(VE, v_r)
                    wait_psum(tensor, key)
                    for k in range(8):
                        inc(tensor.matmul(out=pb[:, co:co + 256],
                                          lhsT=w_deq[:, iw, k, m * 128:(m + 1) * 128],
                                          rhs=rhs_bf[:, k, :],
                                          start=(k == 0), stop=False), "TE")
                    inc(tensor.matmul(out=pb[:, co:co + 256],
                                      lhsT=lb_slice(iw, m * 128, (m + 1) * 128),
                                      rhs=rbf[rb0:rb0 + 32, :],
                                      start=False, stop=True), "TE")
                te_m = cnt["TE"]

                @block.scalar
                def _(scalar, m=m, pb=pb, co=co):
                    scalar.wait_ge(TE, te_m)
                    inc(scalar.activation(out=yv[:, m, :], in_=pb[:, co:co + 256],
                                          func=AF.Identity, bias=cst[:, 112 + iw * 8 + m:113 + iw * 8 + m]), "SE")
                free_psum(key, "SE")
            return cnt["SE"]

        for b in range(B):
            for c in range(NC1):
                ci = ("ffn", b * NC1 + c)
                prev = ("ffn", b * NC1 + c - 1) if (b * NC1 + c - 1) >= 0 else ("ffn", -1)

                @block.sync
                def _(sync, b=b, c=c):
                    chunk_gate(sync, hist[prev])
                    inc(sync.dma_start(
                        out=x_tm[:].rearrange("p (a d) -> p a d", a=2),
                        in_=x_in[b, T1 * c:T1 * (c + 1), :].rearrange("(a p) d -> p a d", p=128)),
                        "LD", 16)
                ldx = cnt["LD"]

                v_xn = emit_ln(ldx, 2)

                te_tp = {}
                for a in range(2):
                    dst = p_tp if a == 0 else p_tp2
                    name = "p_tp" if a == 0 else "p_tp2"
                    srcs = [xn_tm[:, a * 1024 + d * 128:a * 1024 + (d + 1) * 128] for d in range(8)]
                    te_tp[a] = emit_tp(srcs, dst, name, v_xn)

                @block.vector
                def _(vector):
                    vector.wait_ge(TE, te_tp[1])
                    inc(vector.tensor_copy(out=xnT_f[:, :, 0:128], in_=p_tp[:].rearrange("p (d t) -> p d t", d=8)), "VE")
                    inc(vector.tensor_copy(out=xnT_f[:, :, 128:256], in_=p_tp2[:].rearrange("p (d t) -> p d t", d=8)), "VE")

                @block.scalar
                def _(scalar):
                    scalar.wait_ge(TE, te_tp[1])
                    inc(scalar.activation(out=h0[:, :, 0:128], in_=p_tp[:].rearrange("p (d t) -> p d t", d=8),
                                          func=AF.Copy), "SE")
                    inc(scalar.activation(out=h0[:, :, 128:256], in_=p_tp2[:].rearrange("p (d t) -> p d t", d=8),
                                          func=AF.Copy), "SE")
                free_psum("p_tp", "SE", "VE")
                free_psum("p_tp2", "SE", "VE")
                v_rdy = cnt["VE"]

                for blk in range(3):
                    hsrc = h0 if blk % 2 == 0 else h1
                    hdst = h1 if blk % 2 == 0 else h0
                    i0, i1 = 2 * blk, 2 * blk + 1

                    if blk == 0:
                        rhs0 = xnT_f
                        rsem, rv = "VE", v_rdy
                    else:
                        @block.scalar
                        def _(scalar, hsrc=hsrc):
                            scalar.wait_ge(VE, cnt["VE"])
                            inc(scalar.activation(out=hbf[:], in_=hsrc[:], func=AF.Copy), "SE")
                        rhs0 = hbf
                        rsem, rv = "SE", cnt["SE"]

                    s_y1 = ffn_half(i0, rhs0, rsem, rv)

                    @block.scalar
                    def _(scalar):
                        inc(scalar.activation(out=ybf[:], in_=yv[:], func=AF.Copy), "SE")
                        inc(scalar.activation(out=y2bf[:], in_=yv[:], func=AF.Square), "SE")
                    s_cast = cnt["SE"]

                    @block.tensor
                    def _(tensor):
                        tensor.wait_ge(SE, s_cast)
                        wait_psum(tensor, "p_sm")
                        for k in range(8):
                            inc(tensor.matmul(out=p_sm[0:1, 0:256], lhsT=onescol_bf[:],
                                              rhs=ybf[:, k, :], start=(k == 0), stop=(k == 7)), "TE")
                        for k in range(8):
                            inc(tensor.matmul(out=p_sm[0:1, 256:512], lhsT=onescol_bf[:],
                                              rhs=y2bf[:, k, :], start=(k == 0), stop=(k == 7)), "TE")
                    te_mu = cnt["TE"]

                    @block.vector
                    def _(vector):
                        vector.wait_ge(TE, te_mu)
                        inc(vector.tensor_scalar(out=mrs[:, 0:256], in0=p_sm[0:1, 0:256],
                                                 scalar1=R1024, scalar2=None, op0=ALU.mult), "VE")
                        inc(vector.tensor_scalar(out=mrs[:, 256:512], in0=p_sm[0:1, 256:512],
                                                 scalar1=R1024, scalar2=None, op0=ALU.mult), "VE")
                        vdrain(vector)
                        inc(vector.tensor_tensor(out=mrs[:, 512:768], in0=mrs[:, 0:256],
                                                 in1=mrs[:, 0:256], op=ALU.mult), "VE")
                        vdrain(vector)
                        inc(vector.tensor_tensor(out=mrs[:, 256:512], in0=mrs[:, 256:512],
                                                 in1=mrs[:, 512:768], op=ALU.subtract), "VE")
                        vdrain(vector)
                    free_psum("p_sm", "VE")
                    v_vv = cnt["VE"]

                    @block.scalar
                    def _(scalar):
                        scalar.wait_ge(VE, v_vv)
                        inc(scalar.activation(out=mrs[:, 256:512], in_=mrs[:, 256:512],
                                              func=AF.Sqrt, bias=cst[0:1, 464:465]), "SE")
                    s_rsd = cnt["SE"]

                    @block.vector
                    def _(vector):
                        vector.wait_ge(SE, s_rsd)
                        inc(vector.reciprocal(out=mrs[:, 256:512], in_=mrs[:, 256:512]), "VE")
                        vdrain(vector)
                        inc(vector.tensor_copy(out=mrs_bf[:, 0:512], in_=mrs[:, 0:512]), "VE")
                        vdrain(vector)
                    v_mbf = cnt["VE"]

                    @block.tensor
                    def _(tensor):
                        tensor.wait_ge(VE, v_mbf)
                        wait_psum(tensor, "p_ln")
                        inc(tensor.matmul(out=p_ln[:, 0:512], lhsT=ones1_bf[:],
                                          rhs=mrs_bf[:, 0:512], start=True, stop=True), "TE")
                    te_b = cnt["TE"]

                    @block.vector
                    def _(vector):
                        vector.wait_ge(TE, te_b)
                        inc(vector.tensor_tensor(
                            out=yv[:], in0=yv[:],
                            in1=p_ln[:, 0:256].rearrange("p (o t) -> p o t", o=1).broadcast_to([128, 8, 256]),
                            op=ALU.subtract), "VE")
                        inc(vector.tensor_tensor(
                            out=yv[:], in0=yv[:],
                            in1=p_ln[:, 256:512].rearrange("p (o t) -> p o t", o=1).broadcast_to([128, 8, 256]),
                            op=ALU.mult), "VE")
                    free_psum("p_ln", "VE")
                    v_lnd = cnt["VE"]

                    @block.scalar
                    def _(scalar, blk=blk):
                        scalar.wait_ge(VE, v_lnd)
                        for m in range(8):
                            inc(scalar.activation(out=ybf[:, m, :], in_=yv[:, m, :],
                                                  func=AF.Gelu,
                                                  scale=cst[:, 160 + blk * 8 + m:161 + blk * 8 + m],
                                                  bias=cst[:, 184 + blk * 8 + m:185 + blk * 8 + m]), "SE")
                    s_gelu = cnt["SE"]

                    s_y2 = ffn_half(i1, ybf, "SE", s_gelu)

                    @block.vector
                    def _(vector, hsrc=hsrc, hdst=hdst):
                        vector.wait_ge(SE, s_y2)
                        inc(vector.tensor_tensor(out=hdst[:], in0=yv[:], in1=hsrc[:], op=ALU.add), "VE")

                @block.scalar
                def _(scalar):
                    scalar.wait_ge(VE, cnt["VE"])
                    inc(scalar.activation(out=gT_bf[:], in_=h1[:], func=AF.Silu), "SE")
                s_g = cnt["SE"]

                for a in range(2):
                    dst = p_tp if a == 0 else p_tp2
                    name = "p_tp" if a == 0 else "p_tp2"

                    @block.tensor
                    def _(tensor, a=a, dst=dst, name=name):
                        tensor.wait_ge(SE, s_g)
                        wait_psum(tensor, name)
                        for d in range(8):
                            inc(tensor.transpose(out=dst[:, d * 128:(d + 1) * 128],
                                                 in_=gT_bf[:, d, a * 128:(a + 1) * 128],
                                                 identity=id_bf[:]), "TE")
                    te_g = cnt["TE"]

                    @block.vector
                    def _(vector, a=a, dst=dst):
                        vector.wait_ge(TE, te_g)
                        inc(vector.tensor_copy(out=gtm[:, a * 1024:(a + 1) * 1024], in_=dst[:]), "VE")
                    free_psum(name, "VE")
                v_g = cnt["VE"]

                @block.gpsimd
                def _(gpsimd, b=b, c=c):
                    gpsimd.wait_ge(VE, v_g)
                    inc(gpsimd.dma_start(out=g_scr[b, T1 * c:T1 * c + 128, :],
                                         in_=gtm[:, 0:1024]), "ST", 16)
                    inc(gpsimd.dma_start(out=g_scr[b, T1 * c + 128:T1 * c + 256, :],
                                         in_=gtm[:, 1024:2048]), "ST", 16)

                hist[ci] = dict(cnt)

        @block.sync
        def _(sync):
            chunk_gate(sync, dict(cnt))
        s1.close()

        s2a = ExitStack()
        E2 = s2a.enter_context
        winz = E2(nc.sbuf_tensor("winz", [128, 8, 2048], BF16))
        wbufA = E2(nc.sbuf_tensor("wbufA", [128, 2048], F32))
        sz_bf = E2(nc.sbuf_tensor("sz_bf", [128, 16, 128], BF16))

        for k in range(8):
            @block.sync
            def _(sync, k=k):
                if k > 0:
                    sync.wait_ge(VE, cnt["VE"])
                inc(sync.dma_start(out=wbufA[:], in_=w_in[128 * k:128 * (k + 1), DI:2 * DI]), "LD", 16)
            ldw = cnt["LD"]

            @block.vector
            def _(vector, k=k):
                vector.wait_ge(LD, ldw)
                inc(vector.tensor_copy(out=winz[:, k, :], in_=wbufA[:]), "VE")

        hist[("2a", -1)] = dict(cnt)
        for b in range(B):
            for c in range(NC2):
                ci = ("2a", b * NC2 + c)
                prev = ("2a", b * NC2 + c - 1)

                @block.sync
                def _(sync, b=b, c=c):
                    chunk_gate(sync, hist[prev])
                    inc(sync.dma_start(
                        out=x_tm[:, 0:1024],
                        in_=x_in[b, T2 * c:T2 * (c + 1), :]), "LD", 16)
                ldx = cnt["LD"]

                v_xn = emit_ln(ldx, 1)
                srcs = [xn_tm[:, d * 128:(d + 1) * 128] for d in range(8)]
                te_t = emit_tp(srcs, p_tp, "p_tp", v_xn)

                @block.vector
                def _(vector):
                    vector.wait_ge(TE, te_t)
                    inc(vector.tensor_copy(out=xnT_m[:], in_=p_tp[:].rearrange("p (d t) -> p d t", d=8)), "VE")
                free_psum("p_tp", "VE")
                v_x = cnt["VE"]

                for g in range(4):
                    slot = "p_mm" if g % 2 == 0 else "p_mm2"
                    pb = p_mm if g % 2 == 0 else p_mm2

                    @block.tensor
                    def _(tensor, g=g, pb=pb, slot=slot):
                        if g == 0:
                            tensor.wait_ge(VE, v_x)
                        wait_psum(tensor, slot, slot + "0", slot + "1")
                        for ml in range(4):
                            m = 4 * g + ml
                            for k in range(8):
                                inc(tensor.matmul(out=pb[:, ml * 128:(ml + 1) * 128],
                                                  lhsT=winz[:, k, m * 128:(m + 1) * 128],
                                                  rhs=xnT_m[:, k, :],
                                                  start=(k == 0), stop=(k == 7)), "TE")
                    te_z = cnt["TE"]

                    @block.scalar
                    def _(scalar, g=g, pb=pb):
                        scalar.wait_ge(TE, te_z)
                        inc(scalar.activation(out=sz_bf[:, 4 * g:4 * (g + 1), :],
                                              in_=pb[:].rearrange("p (a t) -> p a t", a=4),
                                              func=AF.Silu), "SE")
                    free_psum(slot, "SE")
                s_sz = cnt["SE"]

                @block.gpsimd
                def _(gpsimd, b=b, c=c):
                    gpsimd.wait_ge(SE, s_sz)
                    inc(gpsimd.dma_start(out=sz_scr[b, c], in_=sz_bf[:].rearrange("p a t -> p (a t)")), "ST", 16)

                hist[ci] = dict(cnt)

        @block.sync
        def _(sync):
            chunk_gate(sync, dict(cnt))
        s2a.close()

        s2b = ExitStack()
        E3 = s2b.enter_context
        winx = E3(nc.sbuf_tensor("winx", [128, 8, 2048], BF16))
        wxp_bf = E3(nc.sbuf_tensor("wxp_bf", [128, 16, 96], BF16))
        wdt_bf = E3(nc.sbuf_tensor("wdt_bf", [64, 2048], BF16))
        wout_bf = E3(nc.sbuf_tensor("wout_bf", [128, 16, 1024], BF16))
        wbuf = x_tm
        xpart = E3(nc.sbuf_tensor("xpart", [128, 16, 131], F32))
        xconv = E3(nc.sbuf_tensor("xconv", [128, 16, 128], F32))
        xcv_bf = E3(nc.sbuf_tensor("xcv_bf", [128, 16, 128], BF16))
        dtl_bf = E3(nc.sbuf_tensor("dtl_bf", [64, 128], BF16))
        bc_sb = E3(nc.sbuf_tensor("bc_sb", [32, 128], BF16))
        Brep = E3(nc.sbuf_tensor("Brep", [128, 2048], BF16))
        Crep = E3(nc.sbuf_tensor("Crep", [128, 2048], BF16))
        dt_sb = E3(nc.sbuf_tensor("dt_sb", [128, 16, 128], F32))
        dtx = E3(nc.sbuf_tensor("dtx", [128, 16, 128], BF16))
        a_sc = E3(nc.sbuf_tensor("a_sc", [128, 2064], F32))
        b_sc = E3(nc.sbuf_tensor("b_sc", [128, 2064], F32))
        z_sc = E3(nc.sbuf_tensor("z_sc", [128, 2064], BF16))
        zc_sc = E3(nc.sbuf_tensor("zc_sc", [128, 2048], BF16))
        ytmp = E3(nc.sbuf_tensor("ytmp", [128, 1024], F32))
        ytb = E3(nc.sbuf_tensor("ytb", [128, 128], BF16))
        ybf_m = E3(nc.sbuf_tensor("ybf_m", [128, 16, 128], BF16))
        carry = E3(nc.sbuf_tensor("carry", [128, 256], F32))
        szbuf = E3(nc.sbuf_tensor("szbuf", [128, 16, 128], BF16))
        gbuf = E3(nc.sbuf_tensor("gbuf", [128, 1024], BF16))
        sT_bf = E3(nc.sbuf_tensor("sT_bf", [128, 8, 128], BF16))

        for k in range(8):
            @block.sync
            def _(sync, k=k):
                sync.wait_ge(VE, cnt["VE"])
                inc(sync.dma_start(out=wbuf[:], in_=w_in[128 * k:128 * (k + 1), 0:DI]), "LD", 16)
            ldw = cnt["LD"]

            @block.vector
            def _(vector, k=k):
                vector.wait_ge(LD, ldw)
                inc(vector.tensor_copy(out=winx[:, k, :], in_=wbuf[:]), "VE")

        @block.sync
        def _(sync):
            sync.wait_ge(VE, cnt["VE"])
            inc(sync.dma_start(out=wbuf[:, 0:1536].rearrange("p (t o) -> p t o", t=16),
                               in_=wxp_in.rearrange("(t p) o -> p t o", p=128)), "LD", 16)
        ldw = cnt["LD"]

        @block.vector
        def _(vector):
            vector.wait_ge(LD, ldw)
            inc(vector.tensor_copy(out=wxp_bf[:], in_=wbuf[:, 0:1536].rearrange("p (t o) -> p t o", t=16)), "VE")

        @block.sync
        def _(sync):
            sync.wait_ge(VE, cnt["VE"])
            inc(sync.dma_start(out=wbuf[0:64, 0:2048], in_=wdt_in), "LD", 16)
        ldw = cnt["LD"]

        @block.vector
        def _(vector):
            vector.wait_ge(LD, ldw)
            inc(vector.tensor_copy(out=wdt_bf[:], in_=wbuf[0:64, 0:2048]), "VE")

        for t2 in range(8):
            @block.sync
            def _(sync, t2=t2):
                sync.wait_ge(VE, cnt["VE"])
                inc(sync.dma_start(
                    out=wbuf[:].rearrange("p (a o) -> p a o", a=2),
                    in_=wout_in[256 * t2:256 * (t2 + 1), :].rearrange("(a p) o -> p a o", p=128)), "LD", 16)
            ldw = cnt["LD"]

            @block.vector
            def _(vector, t2=t2):
                vector.wait_ge(LD, ldw)
                inc(vector.tensor_copy(out=wout_bf[:, 2 * t2:2 * t2 + 2, :],
                                       in_=wbuf[:].rearrange("p (a o) -> p a o", a=2)), "VE")

        hist[("2b", -1)] = dict(cnt)
        for b in range(B):
            for c in range(NC2):
                ci = ("2b", b * NC2 + c)
                prev = ("2b", b * NC2 + c - 1)

                @block.sync
                def _(sync, b=b, c=c):
                    chunk_gate(sync, hist[prev])
                    inc(sync.dma_start(out=x_tm[:, 0:1024], in_=x_in[b, T2 * c:T2 * (c + 1), :]), "LD", 16)
                    inc(sync.dma_start(out=szbuf[:], in_=sz_scr[b, c]), "LD", 16)
                    inc(sync.dma_start(out=gbuf[:], in_=g_scr[b, T2 * c:T2 * (c + 1), :]), "LD", 16)
                ldx = cnt["LD"]

                v_xn = emit_ln(ldx, 1)
                srcs = [xn_tm[:, d * 128:(d + 1) * 128] for d in range(8)]
                te_t = emit_tp(srcs, p_tp, "p_tp", v_xn)

                @block.vector
                def _(vector, b=b, c=c):
                    vector.wait_ge(TE, te_t)
                    inc(vector.tensor_copy(out=xnT_m[:], in_=p_tp[:].rearrange("p (d t) -> p d t", d=8)), "VE")
                    if c == 0:
                        inc(vector.memset(xpart[:, :, 0:3], 0.0), "VE")
                        inc(vector.memset(carry[:], 0.0), "VE")
                free_psum("p_tp", "VE")
                v_x = cnt["VE"]

                for g in range(4):
                    slot = "p_mm" if g % 2 == 0 else "p_mm2"
                    pb = p_mm if g % 2 == 0 else p_mm2

                    @block.tensor
                    def _(tensor, g=g, pb=pb, slot=slot):
                        if g == 0:
                            tensor.wait_ge(VE, v_x)
                        wait_psum(tensor, slot, slot + "0", slot + "1")
                        for ml in range(4):
                            m = 4 * g + ml
                            for k in range(8):
                                inc(tensor.matmul(out=pb[:, ml * 128:(ml + 1) * 128],
                                                  lhsT=winx[:, k, m * 128:(m + 1) * 128],
                                                  rhs=xnT_m[:, k, :],
                                                  start=(k == 0), stop=(k == 7)), "TE")
                    te_xz = cnt["TE"]

                    @block.vector
                    def _(vector, g=g, pb=pb):
                        vector.wait_ge(TE, te_xz)
                        inc(vector.tensor_copy(out=xpart[:, 4 * g:4 * (g + 1), 3:131],
                                               in_=pb[:].rearrange("p (a t) -> p a t", a=4)), "VE")
                    free_psum(slot, "VE")

                @block.vector
                def _(vector):
                    for t in range(16):
                        inc(vector.tensor_scalar(out=xconv[:, t, :], in0=xpart[:, t, 0:128],
                                                 scalar1=cst[:, 4 * t:4 * t + 1], scalar2=None,
                                                 op0=ALU.mult), "VE")
                        for kk in range(1, 4):
                            vdrain(vector)
                            inc(vector.scalar_tensor_tensor(out=xconv[:, t, :], in0=xpart[:, t, kk:kk + 128],
                                                            scalar=cst[:, 4 * t + kk:4 * t + kk + 1],
                                                            in1=xconv[:, t, :],
                                                            op0=ALU.mult, op1=ALU.add), "VE")
                    inc(vector.tensor_copy(out=xpart[:, :, 0:3], in_=xpart[:, :, 128:131]), "VE")
                v_cv = cnt["VE"]

                @block.scalar
                def _(scalar):
                    scalar.wait_ge(VE, v_cv)
                    for t in range(16):
                        inc(scalar.activation(out=xconv[:, t, :], in_=xconv[:, t, :],
                                              func=AF.Silu, bias=cst[:, 64 + t:65 + t]), "SE")
                    inc(scalar.activation(out=xcv_bf[:], in_=xconv[:], func=AF.Copy), "SE")
                s_cv = cnt["SE"]

                @block.tensor
                def _(tensor):
                    tensor.wait_ge(SE, s_cv)
                    wait_psum(tensor, "p_sm")
                    for k in range(16):
                        inc(tensor.matmul(out=p_sm[0:96, 0:128], lhsT=wxp_bf[:, k, :],
                                          rhs=xcv_bf[:, k, :], start=(k == 0), stop=(k == 15)), "TE")
                te_pj = cnt["TE"]

                @block.vector
                def _(vector):
                    vector.wait_ge(TE, te_pj)
                    inc(vector.tensor_copy(out=dtl_bf[:], in_=p_sm[0:64, 0:128]), "VE")
                    inc(vector.tensor_copy(out=bc_sb[:], in_=p_sm[64:96, 0:128]), "VE")
                free_psum("p_sm", "VE")
                v_pj = cnt["VE"]

                @block.gpsimd
                def _(gpsimd):
                    gpsimd.wait_ge(VE, v_pj)
                    inc(gpsimd.dma_start(out=bc_scr, in_=bc_sb[:]), "ST", 16)
                st_bc = cnt["ST"]

                @block.sync
                def _(sync):
                    sync.wait_ge(ST, st_bc)
                    inc(sync.dma_start(out=gtm[0:1, 0:2048],
                                       in_=bc_scr[0:16, :].rearrange("s t -> (s t)")), "LD", 16)
                    inc(sync.dma_start(out=xn_tm[0:1, 0:2048],
                                       in_=bc_scr[16:32, :].rearrange("s t -> (s t)")), "LD", 16)
                ld_bc = cnt["LD"]

                for piece in range(4):
                    slot = "p_rep" if piece % 2 == 0 else "p_ln"
                    pb = p_rep if piece % 2 == 0 else p_ln

                    @block.tensor
                    def _(tensor, piece=piece, pb=pb, slot=slot):
                        tensor.wait_ge(LD, ld_bc)
                        wait_psum(tensor, slot)
                        inc(tensor.matmul(out=pb[:, 0:512], lhsT=ones1_bf[:],
                                          rhs=gtm[0:1, 512 * piece:512 * (piece + 1)],
                                          start=True, stop=True), "TE")
                    te_r = cnt["TE"]

                    @block.vector
                    def _(vector, piece=piece, pb=pb):
                        vector.wait_ge(TE, te_r)
                        inc(vector.tensor_copy(out=Brep[:, 512 * piece:512 * (piece + 1)],
                                               in_=pb[:, 0:512]), "VE")
                    free_psum(slot, "VE")
                for piece in range(4):
                    slot = "p_rep" if piece % 2 == 0 else "p_ln"
                    pb = p_rep if piece % 2 == 0 else p_ln

                    @block.tensor
                    def _(tensor, piece=piece, pb=pb, slot=slot):
                        wait_psum(tensor, slot)
                        inc(tensor.matmul(out=pb[:, 0:512], lhsT=ones1_bf[:],
                                          rhs=xn_tm[0:1, 512 * piece:512 * (piece + 1)],
                                          start=True, stop=True), "TE")
                    te_r = cnt["TE"]

                    @block.vector
                    def _(vector, piece=piece, pb=pb):
                        vector.wait_ge(TE, te_r)
                        inc(vector.tensor_copy(out=Crep[:, 512 * piece:512 * (piece + 1)],
                                               in_=pb[:, 0:512]), "VE")
                    free_psum(slot, "VE")

                for g in range(4):
                    slot = "p_mm" if g % 2 == 0 else "p_mm2"
                    pb = p_mm if g % 2 == 0 else p_mm2

                    @block.tensor
                    def _(tensor, g=g, pb=pb, slot=slot):
                        wait_psum(tensor, slot, slot + "0", slot + "1")
                        for ml in range(4):
                            m = 4 * g + ml
                            inc(tensor.matmul(out=pb[:, ml * 128:(ml + 1) * 128],
                                              lhsT=wdt_bf[:, m * 128:(m + 1) * 128],
                                              rhs=dtl_bf[:], start=True, stop=True), "TE")
                    te_d = cnt["TE"]

                    @block.scalar
                    def _(scalar, g=g, pb=pb):
                        scalar.wait_ge(TE, te_d)
                        for ml in range(4):
                            m = 4 * g + ml
                            inc(scalar.activation(out=dt_sb[:, m, :], in_=pb[:, ml * 128:(ml + 1) * 128],
                                                  func=AF.Exp, bias=cst[:, 80 + m:81 + m]), "SE")
                    free_psum(slot, "SE")

                @block.scalar
                def _(scalar):
                    sdrain(scalar)
                    inc(scalar.activation(out=dt_sb[:].rearrange("p t e -> p (t e)"),
                                          in_=dt_sb[:].rearrange("p t e -> p (t e)"),
                                          func=AF.Ln, bias=1.0), "SE")
                s_dt = cnt["SE"]

                @block.vector
                def _(vector):
                    vector.wait_ge(SE, s_dt)
                    inc(vector.tensor_tensor(out=dtx[:].rearrange("p t e -> p (t e)"),
                                             in0=dt_sb[:].rearrange("p t e -> p (t e)"),
                                             in1=xconv[:].rearrange("p t e -> p (t e)"),
                                             op=ALU.mult), "VE")
                v_dtx = cnt["VE"]

                a3 = a_sc[:].rearrange("p (s e) -> p s e", s=16)
                b3 = b_sc[:].rearrange("p (s e) -> p s e", s=16)
                z3 = z_sc[:].rearrange("p (s e) -> p s e", s=16)
                zc3 = zc_sc[:].rearrange("p (s e) -> p s e", s=16)
                for til in range(16):
                    @block.scalar
                    def _(scalar, til=til):
                        scalar.wait_ge(VE, cnt["VE"])
                        for s in range(16):
                            inc(scalar.activation(out=a3[:, s, 1:129], in_=dt_sb[:, til, :],
                                                  func=AF.Exp,
                                                  scale=acol[:, til * 16 + s:til * 16 + s + 1]), "SE")
                    s_a = cnt["SE"]

                    @block.vector
                    def _(vector, til=til):
                        vector.wait_ge(SE, s_a)
                        inc(vector.memset(a3[:, :, 0:1], 0.0), "VE")
                        inc(vector.tensor_tensor(
                            out=b3[:, :, 1:129],
                            in0=dtx[:, til, :].rearrange("p (o e) -> p o e", o=1).broadcast_to([128, 16, 128]),
                            in1=Brep[:].rearrange("p (s e) -> p s e", s=16), op=ALU.mult), "VE")
                        inc(vector.tensor_copy(out=b3[:, :, 0], in_=carry[:, til * 16:(til + 1) * 16]), "VE")
                        vdrain(vector)
                        inc(vector.tensor_tensor_scan(out=z_sc[:], data0=a_sc[:], data1=b_sc[:],
                                                      initial=0.0, op0=ALU.mult, op1=ALU.add), "VE")
                        vdrain(vector)
                        inc(vector.tensor_copy(out=carry[:, til * 16:(til + 1) * 16], in_=z3[:, :, 128]), "VE")
                        inc(vector.tensor_tensor(out=zc3[:], in0=z3[:, :, 1:129],
                                                 in1=Crep[:].rearrange("p (s e) -> p s e", s=16),
                                                 op=ALU.mult), "VE")
                        inc(vector.tensor_tensor(out=ytmp[:, 0:1024], in0=zc_sc[:, 0:1024],
                                                 in1=zc_sc[:, 1024:2048], op=ALU.add), "VE")
                        vdrain(vector)
                        inc(vector.tensor_tensor(out=ytmp[:, 0:512], in0=ytmp[:, 0:512],
                                                 in1=ytmp[:, 512:1024], op=ALU.add), "VE")
                        vdrain(vector)
                        inc(vector.tensor_tensor(out=ytmp[:, 0:256], in0=ytmp[:, 0:256],
                                                 in1=ytmp[:, 256:512], op=ALU.add), "VE")
                        vdrain(vector)
                        inc(vector.tensor_tensor(out=ytmp[:, 0:128], in0=ytmp[:, 0:128],
                                                 in1=ytmp[:, 128:256], op=ALU.add), "VE")
                        vdrain(vector)
                        inc(vector.scalar_tensor_tensor(out=ytb[:], in0=xconv[:, til, :],
                                                        scalar=cst[:, 96 + til:97 + til],
                                                        in1=ytmp[:, 0:128],
                                                        op0=ALU.mult, op1=ALU.add), "VE")
                        vdrain(vector)
                        inc(vector.tensor_tensor(out=ybf_m[:, til, :], in0=ytb[:],
                                                 in1=szbuf[:, til, :], op=ALU.mult), "VE")

                v_scan = cnt["VE"]

                for g in range(2):
                    slot = "p_mm" if g % 2 == 0 else "p_mm2"
                    pb = p_mm if g % 2 == 0 else p_mm2

                    @block.tensor
                    def _(tensor, g=g, pb=pb, slot=slot):
                        tensor.wait_ge(VE, v_scan)
                        wait_psum(tensor, slot, slot + "0", slot + "1")
                        for ml in range(4):
                            m = 4 * g + ml
                            for k in range(16):
                                inc(tensor.matmul(out=pb[:, ml * 128:(ml + 1) * 128],
                                                  lhsT=wout_bf[:, k, m * 128:(m + 1) * 128],
                                                  rhs=ybf_m[:, k, :],
                                                  start=(k == 0), stop=(k == 15)), "TE")
                    te_o = cnt["TE"]

                    @block.scalar
                    def _(scalar, g=g, pb=pb):
                        scalar.wait_ge(TE, te_o)
                        inc(scalar.activation(out=sT_bf[:, 4 * g:4 * (g + 1), :],
                                              in_=pb[:].rearrange("p (a t) -> p a t", a=4),
                                              func=AF.Copy), "SE")
                    free_psum(slot, "SE")
                s_o = cnt["SE"]

                @block.tensor
                def _(tensor):
                    tensor.wait_ge(SE, s_o)
                    wait_psum(tensor, "p_tp2")
                    for d in range(8):
                        inc(tensor.transpose(out=p_tp2[:, d * 128:(d + 1) * 128],
                                             in_=sT_bf[:, d, :], identity=id_bf[:]), "TE")
                te_s = cnt["TE"]

                @block.vector
                def _(vector):
                    vector.wait_ge(TE, te_s)
                    inc(vector.tensor_tensor(out=cmb[:], in0=p_tp2[:], in1=gbuf[:], op=ALU.mult), "VE")
                    inc(vector.tensor_tensor(out=out_tm[:], in0=cmb[:], in1=x_tm[:, 0:1024], op=ALU.add), "VE")
                free_psum("p_tp2", "VE")
                v_out = cnt["VE"]

                @block.gpsimd
                def _(gpsimd, b=b, c=c):
                    gpsimd.wait_ge(VE, v_out)
                    inc(gpsimd.dma_start(out=out_hbm[b, T2 * c:T2 * (c + 1), :], in_=out_tm[:]), "ST", 16)

                hist[ci] = dict(cnt)

        @block.sync
        def _(sync):
            sync.wait_ge(TE, cnt["TE"])
            sync.wait_ge(VE, cnt["VE"])
            sync.wait_ge(SE, cnt["SE"])
            sync.wait_ge(ST, cnt["ST"])
            sync.wait_ge(LD, cnt["LD"])
        s2b.close()
    return nc


_NC_CACHE = None


def kernel(L0_attn_hidden_states, L0_attn_W_in, L0_attn_conv_w, L0_attn_conv_b,
           L0_attn_W_x, L0_attn_W_dt, L0_attn_b_dt, L0_attn_A_log, L0_attn_D,
           L0_attn_W_out, L0_ffn_qweights, L0_ffn_scales, L0_ffn_biases,
           L0_ffn_lora_a, L0_ffn_lora_b, L0_ffn_ln_gamma, L0_ffn_ln_beta):
    global _NC_CACHE
    x = np.ascontiguousarray(np.asarray(L0_attn_hidden_states, dtype=np.float32))
    f = lambda a: np.ascontiguousarray(np.asarray(a, dtype=np.float32))
    w_in = f(L0_attn_W_in)
    conv_w = f(L0_attn_conv_w)
    conv_b = f(L0_attn_conv_b)
    wxp = f(L0_attn_W_x)
    wdt = f(L0_attn_W_dt)
    b_dt = f(L0_attn_b_dt)
    alog = f(L0_attn_A_log)
    dD = f(L0_attn_D)
    wout = f(L0_attn_W_out)
    q = np.ascontiguousarray(np.asarray(L0_ffn_qweights, dtype=np.int32))
    sc = f(L0_ffn_scales)
    bias = f(L0_ffn_biases)
    la = f(L0_ffn_lora_a)
    lb = f(L0_ffn_lora_b)
    gam = f(L0_ffn_ln_gamma)
    bet = f(L0_ffn_ln_beta)

    qT = np.ascontiguousarray(q.transpose(0, 2, 1))
    scT = np.ascontiguousarray(sc.transpose(0, 2, 1))
    laT = np.ascontiguousarray(la.transpose(0, 2, 1))
    lbT = np.ascontiguousarray(lb.transpose(0, 2, 1))
    convw_c = np.ascontiguousarray(conv_w.T.reshape(16, 128, 4).transpose(1, 0, 2).reshape(128, 64))
    convb_c = np.ascontiguousarray(conv_b.reshape(16, 128).T)
    bdt_c = np.ascontiguousarray(b_dt.reshape(16, 128).T)
    dD_c = np.ascontiguousarray(dD.reshape(16, 128).T)
    alog_c = np.ascontiguousarray(alog.reshape(16, 128, 16).transpose(1, 0, 2).reshape(128, 256))
    bias_c = np.ascontiguousarray(bias.reshape(6, 8, 128).transpose(2, 0, 1).reshape(128, 48))
    gam_c = np.ascontiguousarray(gam.reshape(3, 8, 128).transpose(2, 0, 1).reshape(128, 24))
    bet_c = np.ascontiguousarray(bet.reshape(3, 8, 128).transpose(2, 0, 1).reshape(128, 24))
    id_c = np.eye(128, dtype=np.float32)
    ones16 = np.ones((16, 128), dtype=np.float32)
    ones1 = np.ones((1, 128), dtype=np.float32)
    onescol = np.ones((128, 1), dtype=np.float32)
    e64 = np.zeros((64, 8, 128), dtype=np.float32)
    for t in range(8):
        for p in range(128):
            e64[8 * t + p // 16, t, p] = 1.0
    e64 = np.ascontiguousarray(np.concatenate([e64.reshape(64, 1024)] * 2, axis=0))

    if _NC_CACHE is None:
        _NC_CACHE = build_nc()
    nc = _NC_CACHE

    shared = dict(w_in=w_in, wxp_in=wxp, wdt_in=wdt, wout_in=wout,
                  convw_in=convw_c, convb_in=convb_c, bdt_in=bdt_c,
                  alog_in=alog_c, dD_in=dD_c, qT_in=qT, scT_in=scT,
                  bias_in=bias_c, laT_in=laT, lbT_in=lbT, gam_in=gam_c,
                  bet_in=bet_c, id_in=id_c, ones16_in=ones16, ones1_in=ones1,
                  onescol_in=onescol, e64_in=e64)
    in_maps = []
    for core in range(8):
        m = dict(shared)
        m["x_in"] = x[2 * core:2 * core + 2]
        in_maps.append(m)

    res = run_bass_kernel_spmd(nc, in_maps, core_ids=list(range(8)))
    outs = [res.results[i]["out"] for i in range(8)]
    full = np.ascontiguousarray(np.concatenate(outs, axis=0)).astype(np.float32)
    try:
        np.save("/tmp/kout.npy", full)
        np.save("/tmp/kg.npy", np.asarray(res.results[0]["g_scr"], dtype=np.float32))
        np.save("/tmp/ksz.npy", np.asarray(res.results[0]["sz_scr"], dtype=np.float32))
    except Exception as e:
        print("dump err", e)
    return full


# version 09 (final)
# speedup vs baseline: 1.1490x; 1.1490x over previous
import numpy as np
from contextlib import ExitStack
import concourse.bass as bass
import concourse.mybir as mybir
from concourse.bass_utils import run_bass_kernel_spmd

F32 = mybir.dt.float32
BF16 = mybir.dt.bfloat16
I32 = mybir.dt.int32
AF = mybir.ActivationFunctionType
ALU = mybir.AluOpType

B = 2
L = 2048
D = 1024
DI = 2048
DS = 16
DTR = 64
T1 = 256
T2 = 128
NC1 = L // T1
NC2 = L // T2
EPS = 1e-5
R1024 = 1.0 / 1024.0


def build_nc():
    nc = bass.Bass("TRN2", target_bir_lowering=False, detect_race_conditions=False)

    x_in = nc.dram_tensor("x_in", [B, L, D], F32, kind="ExternalInput").ap()
    w_in = nc.dram_tensor("w_in", [D, 2 * DI], F32, kind="ExternalInput").ap()
    wxp_in = nc.dram_tensor("wxp_in", [DI, DTR + 2 * DS], F32, kind="ExternalInput").ap()
    wdt_in = nc.dram_tensor("wdt_in", [DTR, DI], F32, kind="ExternalInput").ap()
    wout_in = nc.dram_tensor("wout_in", [DI, D], F32, kind="ExternalInput").ap()
    convw_in = nc.dram_tensor("convw_in", [128, 64], F32, kind="ExternalInput").ap()
    convb_in = nc.dram_tensor("convb_in", [128, 16], F32, kind="ExternalInput").ap()
    bdt_in = nc.dram_tensor("bdt_in", [128, 16], F32, kind="ExternalInput").ap()
    alog_in = nc.dram_tensor("alog_in", [128, 256], F32, kind="ExternalInput").ap()
    dD_in = nc.dram_tensor("dD_in", [128, 16], F32, kind="ExternalInput").ap()
    qT_in = nc.dram_tensor("qT_in", [6, D, D], I32, kind="ExternalInput").ap()
    scT_in = nc.dram_tensor("scT_in", [6, 64, D], F32, kind="ExternalInput").ap()
    bias_in = nc.dram_tensor("bias_in", [128, 48], F32, kind="ExternalInput").ap()
    laT_in = nc.dram_tensor("laT_in", [6, D, 32], F32, kind="ExternalInput").ap()
    lbT_in = nc.dram_tensor("lbT_in", [6, 32, D], F32, kind="ExternalInput").ap()
    gam_in = nc.dram_tensor("gam_in", [128, 24], F32, kind="ExternalInput").ap()
    bet_in = nc.dram_tensor("bet_in", [128, 24], F32, kind="ExternalInput").ap()
    id_in = nc.dram_tensor("id_in", [128, 128], F32, kind="ExternalInput").ap()
    ones16_in = nc.dram_tensor("ones16_in", [16, 128], F32, kind="ExternalInput").ap()
    ones1_in = nc.dram_tensor("ones1_in", [1, 128], F32, kind="ExternalInput").ap()
    onescol_in = nc.dram_tensor("onescol_in", [128, 1], F32, kind="ExternalInput").ap()
    e64_in = nc.dram_tensor("e64_in", [128, 1024], F32, kind="ExternalInput").ap()

    out_hbm = nc.dram_tensor("out", [B, L, D], F32, kind="ExternalOutput").ap()
    g_scr = nc.dram_tensor("g_scr", [B, L, D], BF16, kind="ExternalOutput").ap()
    sz_scr = nc.dram_tensor("sz_scr", [B, NC2, 128, DI], BF16, kind="ExternalOutput").ap()
    bc_scr = nc.dram_tensor("bc_scr", [32, 128], BF16, kind="Internal").ap()

    es = ExitStack()
    EN = es.enter_context
    with es:
        LD = EN(nc.semaphore("LD"))
        ST = EN(nc.semaphore("ST"))
        TE = EN(nc.semaphore("TE"))
        VE = EN(nc.semaphore("VE"))
        SE = EN(nc.semaphore("SE"))
        cnt = {"LD": 0, "ST": 0, "TE": 0, "VE": 0, "SE": 0}
        SEM = {"LD": LD, "ST": ST, "TE": TE, "VE": VE, "SE": SE}
        pfree = {}

        def inc(instr, s, k=1):
            cnt[s] += k
            instr.then_inc(SEM[s], k)

        def wait_psum(eng, *names):
            for n in names:
                for s, v in pfree.get(n, {}).items():
                    eng.wait_ge(SEM[s], v)

        def free_psum(name, *sems):
            d = pfree.setdefault(name, {})
            for s in sems:
                d[s] = cnt[s]

        def vdrain(vector):
            vector.wait_ge(VE, cnt["VE"])

        def sdrain(scalar):
            scalar.wait_ge(SE, cnt["SE"])

        x_tm = EN(nc.sbuf_tensor("x_tm", [128, 2048], F32))
        xn_tm = EN(nc.sbuf_tensor("xn_tm", [128, 2048], BF16))
        sq = EN(nc.sbuf_tensor("sq", [128, 2048], F32))
        mu_sb = EN(nc.sbuf_tensor("mu_sb", [128, 2], F32))
        m2_sb = EN(nc.sbuf_tensor("m2_sb", [128, 2], F32))
        rs_sb = EN(nc.sbuf_tensor("rs_sb", [128, 2], F32))
        mu2_sb = EN(nc.sbuf_tensor("mu2_sb", [128, 2], F32))
        xnT_m = EN(nc.sbuf_tensor("xnT_m", [128, 8, 128], BF16))
        out_tm = EN(nc.sbuf_tensor("out_tm", [128, 1024], F32))
        cmb = EN(nc.sbuf_tensor("cmb", [128, 1024], F32))
        gtm = EN(nc.sbuf_tensor("gtm", [128, 2048], BF16))
        id_bf = EN(nc.sbuf_tensor("id_bf", [128, 128], BF16))
        ones16_bf = EN(nc.sbuf_tensor("ones16_bf", [16, 128], BF16))
        ones1_bf = EN(nc.sbuf_tensor("ones1_bf", [1, 128], BF16))
        onescol_bf = EN(nc.sbuf_tensor("onescol_bf", [128, 1], BF16))
        e64_bf = EN(nc.sbuf_tensor("e64_bf", [128, 1024], BF16))
        cst = EN(nc.sbuf_tensor("cst", [128, 512], F32))
        acol = EN(nc.sbuf_tensor("acol", [128, 256], F32))
        mrs = EN(nc.sbuf_tensor("mrs", [1, 768], F32))
        mrs_bf = EN(nc.sbuf_tensor("mrs_bf", [1, 512], BF16))

        p_mm = EN(nc.psum_tensor("p_mm", [128, 512], F32))
        p_mm2 = EN(nc.psum_tensor("p_mm2", [128, 512], F32))
        p_rep = EN(nc.psum_tensor("p_rep", [128, 512], F32))
        p_ln = EN(nc.psum_tensor("p_ln", [128, 512], F32))
        p_sm = EN(nc.psum_tensor("p_sm", [128, 512], F32))
        p_tp = EN(nc.psum_tensor("p_tp", [128, 1024], BF16))
        p_tp2 = EN(nc.psum_tensor("p_tp2", [128, 1024], BF16))

        block = EN(nc.Block())

        @block.sync
        def _(sync):
            for src, dst in [(convw_in, cst[:, 0:64]), (convb_in, cst[:, 64:80]),
                             (bdt_in, cst[:, 80:96]), (dD_in, cst[:, 96:112]),
                             (bias_in, cst[:, 112:160]), (gam_in, cst[:, 160:184]),
                             (bet_in, cst[:, 184:208]), (alog_in, cst[:, 208:464]),
                             (id_in, sq[:, 0:128]), (ones16_in, sq[0:16, 128:256]),
                             (ones1_in, sq[0:1, 256:384]), (onescol_in, sq[:, 384:385]),
                             (e64_in, x_tm[:, 0:1024])]:
                inc(sync.dma_start(out=dst, in_=src), "LD", 16)
        ld0 = cnt["LD"]

        @block.scalar
        def _(scalar):
            scalar.wait_ge(LD, ld0)
            inc(scalar.activation(out=acol[:], in_=cst[:, 208:464], func=AF.Exp), "SE")
        se0 = cnt["SE"]

        @block.vector
        def _(vector):
            vector.wait_ge(LD, ld0)
            vector.tensor_copy(out=id_bf[:], in_=sq[:, 0:128])
            vector.tensor_copy(out=ones16_bf[:], in_=sq[0:16, 128:256])
            vector.tensor_copy(out=ones1_bf[:], in_=sq[0:1, 256:384])
            vector.tensor_copy(out=onescol_bf[:], in_=sq[:, 384:385])
            vector.tensor_copy(out=e64_bf[:], in_=x_tm[:, 0:1024])
            vector.memset(cst[:, 464:465], EPS)
            vector.wait_ge(SE, se0)
            inc(vector.tensor_scalar(out=acol[:], in0=acol[:], scalar1=-1.0,
                                     scalar2=None, op0=ALU.mult), "VE")

        hist = {-1: dict(cnt)}

        def chunk_gate(sync, h):
            sync.wait_ge(TE, h["TE"])
            sync.wait_ge(VE, h["VE"])
            sync.wait_ge(SE, h["SE"])
            sync.wait_ge(ST, h["ST"])

        def emit_ln(ld_need, ntt, ve_extra=None):
            @block.vector
            def _(vector):
                vector.wait_ge(LD, ld_need)
                if ve_extra is not None:
                    vector.wait_ge(VE, ve_extra)
                inc(vector.reduce_sum(
                    out=mu_sb[:, 0:ntt],
                    in_=x_tm[:, 0:ntt * 1024].rearrange("p (a d) -> p a d", a=ntt),
                    axis=mybir.AxisListType.X), "VE")
                vdrain(vector)
                inc(vector.tensor_scalar(out=mu_sb[:, 0:ntt], in0=mu_sb[:, 0:ntt],
                                         scalar1=R1024, scalar2=None, op0=ALU.mult), "VE")
                vdrain(vector)
            v_mu = cnt["VE"]

            @block.scalar
            def _(scalar):
                scalar.wait_ge(LD, ld_need)
                inc(scalar.activation(out=sq[:, 0:ntt * 1024], in_=x_tm[:, 0:ntt * 1024],
                                      func=AF.Square), "SE")
            s_sq = cnt["SE"]

            @block.vector
            def _(vector):
                vector.wait_ge(SE, s_sq)
                inc(vector.reduce_sum(
                    out=m2_sb[:, 0:ntt],
                    in_=sq[:, 0:ntt * 1024].rearrange("p (a d) -> p a d", a=ntt),
                    axis=mybir.AxisListType.X), "VE")
                vdrain(vector)
                inc(vector.tensor_scalar(out=m2_sb[:, 0:ntt], in0=m2_sb[:, 0:ntt],
                                         scalar1=R1024, scalar2=None, op0=ALU.mult), "VE")
                vdrain(vector)
                inc(vector.tensor_tensor(out=mu2_sb[:, 0:ntt], in0=mu_sb[:, 0:ntt],
                                         in1=mu_sb[:, 0:ntt], op=ALU.mult), "VE")
                vdrain(vector)
                inc(vector.tensor_tensor(out=rs_sb[:, 0:ntt], in0=m2_sb[:, 0:ntt],
                                         in1=mu2_sb[:, 0:ntt], op=ALU.subtract), "VE")
                vdrain(vector)
            v_var = cnt["VE"]

            @block.scalar
            def _(scalar):
                scalar.wait_ge(VE, v_var)
                inc(scalar.activation(out=rs_sb[:, 0:ntt], in_=rs_sb[:, 0:ntt],
                                      func=AF.Sqrt, bias=cst[:, 464:465]), "SE")
            s_rs = cnt["SE"]

            @block.vector
            def _(vector):
                vector.wait_ge(SE, s_rs)
                inc(vector.reciprocal(out=rs_sb[:, 0:ntt], in_=rs_sb[:, 0:ntt]), "VE")
                vdrain(vector)
                for a in range(ntt):
                    inc(vector.tensor_scalar(
                        out=xn_tm[:, a * 1024:(a + 1) * 1024],
                        in0=x_tm[:, a * 1024:(a + 1) * 1024],
                        scalar1=mu_sb[:, a:a + 1], scalar2=rs_sb[:, a:a + 1],
                        op0=ALU.subtract, op1=ALU.mult), "VE")
            return cnt["VE"]

        def emit_tp(src_slices, dst_psum, dst_name, wait_ve):
            @block.tensor
            def _(tensor):
                tensor.wait_ge(VE, wait_ve)
                wait_psum(tensor, dst_name)
                for d, srcap in enumerate(src_slices):
                    inc(tensor.transpose(out=dst_psum[:, d * 128:(d + 1) * 128],
                                         in_=srcap, identity=id_bf[:]), "TE")
            return cnt["TE"]

        s1 = ExitStack()
        E1 = s1.enter_context
        w_deq = E1(nc.sbuf_tensor("w_deq", [128, 6, 8, 1024], BF16))
        la_sb = E1(nc.sbuf_tensor("la_sb", [128, 6, 256], BF16))
        lb_sb = E1(nc.sbuf_tensor("lb_sb", [128, 2, 1024], BF16))
        qbuf = E1(nc.sbuf_tensor("qbuf", [128, 1024], I32))
        qf = E1(nc.sbuf_tensor("qf", [128, 2048], F32))
        scT_sb = E1(nc.sbuf_tensor("scT_sb", [128, 3, 1024], BF16))
        xnT_f = E1(nc.sbuf_tensor("xnT_f", [128, 8, 256], BF16))
        h0 = E1(nc.sbuf_tensor("h0", [128, 8, 256], F32))
        h1 = E1(nc.sbuf_tensor("h1", [128, 8, 256], F32))
        hbf = E1(nc.sbuf_tensor("hbf", [128, 8, 256], BF16))
        yv = E1(nc.sbuf_tensor("yv", [128, 8, 256], F32))
        ybf = E1(nc.sbuf_tensor("ybf", [128, 8, 256], BF16))
        y2bf = E1(nc.sbuf_tensor("y2bf", [128, 8, 256], BF16))
        rbf = E1(nc.sbuf_tensor("rbf", [128, 256], BF16))
        gT_bf = hbf

        for j in range(6):
            @block.sync
            def _(sync, j=j):
                sync.wait_ge(VE, cnt["VE"])
                inc(sync.dma_start(out=x_tm[0:64, 0:1024], in_=scT_in[j]), "LD", 16)
                inc(sync.dma_start(
                    out=qf[:, 1024:1280].rearrange("p (t r) -> p t r", t=8),
                    in_=laT_in[j].rearrange("(t p) r -> p t r", p=128)), "LD", 16)
                inc(sync.dma_start(out=x_tm[0:32, 1024:2048], in_=lbT_in[j]), "LD", 16)
            ldj = cnt["LD"]

            @block.vector
            def _(vector, j=j):
                vector.wait_ge(LD, ldj)
                inc(vector.tensor_copy(out=scT_sb[64 * (j % 2):64 * (j % 2) + 64, j // 2, :],
                                       in_=x_tm[0:64, 0:1024]), "VE")
                inc(vector.tensor_copy(out=la_sb[:, j, :], in_=qf[:, 1024:1280]), "VE")
                inc(vector.tensor_copy(out=lb_sb[32 * (j % 3):32 * (j % 3) + 32, j // 3, :],
                                       in_=x_tm[0:32, 1024:2048]), "VE")

        def sc_slice(j, c0, c1):
            return scT_sb[64 * (j % 2):64 * (j % 2) + 64, j // 2, c0:c1]

        def lb_slice(j, c0, c1):
            return lb_sb[32 * (j % 3):32 * (j % 3) + 32, j // 3, c0:c1]

        for j in range(6):
            for t in range(8):
                @block.sync
                def _(sync):
                    sync.wait_ge(VE, cnt["VE"])
                    sync.wait_ge(SE, cnt["SE"])
                    inc(sync.dma_start(out=qbuf[:], in_=qT_in[j, 128 * t:128 * (t + 1), :]),
                        "LD", 16)
                ldq = cnt["LD"]

                @block.vector
                def _(vector):
                    vector.wait_ge(LD, ldq)
                    inc(vector.tensor_copy(out=qf[:, 0:1024], in_=qbuf[:]), "VE")
                vq = cnt["VE"]

                @block.scalar
                def _(scalar):
                    scalar.wait_ge(VE, vq)
                    inc(scalar.activation(out=qf[:, 0:1024], in_=qf[:, 0:1024], func=AF.Copy,
                                          scale=1.0 / 7.5, bias=-1.0), "SE")
                sqd = cnt["SE"]

                @block.tensor
                def _(tensor, j=j, t=t):
                    tensor.wait_ge(SE, sqd)
                    wait_psum(tensor, "p_rep", "p_mm")
                    p0 = 64 * (j % 2)
                    inc(tensor.matmul(out=p_rep[:, 0:512],
                                      lhsT=e64_bf[p0:p0 + 64, 128 * t:128 * (t + 1)],
                                      rhs=sc_slice(j, 0, 512),
                                      start=True, stop=True), "TE")
                    inc(tensor.matmul(out=p_mm[:, 0:512],
                                      lhsT=e64_bf[p0:p0 + 64, 128 * t:128 * (t + 1)],
                                      rhs=sc_slice(j, 512, 1024),
                                      start=True, stop=True), "TE")
                ter = cnt["TE"]

                @block.vector
                def _(vector, j=j, t=t):
                    vector.wait_ge(TE, ter)
                    inc(vector.tensor_tensor(out=w_deq[:, j, t, 0:512],
                                             in0=qf[:, 0:512],
                                             in1=p_rep[:, 0:512], op=ALU.mult), "VE")
                    inc(vector.tensor_tensor(out=w_deq[:, j, t, 512:1024],
                                             in0=qf[:, 512:1024],
                                             in1=p_mm[:, 0:512], op=ALU.mult), "VE")
                free_psum("p_rep", "VE")
                free_psum("p_mm", "VE")

        hist[("ffn", -1)] = dict(cnt)

        def ffn_half(iw, rhs_bf, rhs_ready_sem, rhs_ready):
            @block.tensor
            def _(tensor):
                tensor.wait_ge(SEM[rhs_ready_sem], rhs_ready)
                wait_psum(tensor, "p_sm")
                for k in range(8):
                    inc(tensor.matmul(out=p_sm[0:32, 0:256],
                                      lhsT=la_sb[:, iw, :].rearrange("p (t r) -> p t r", t=8)[:, k, :],
                                      rhs=rhs_bf[:, k, :],
                                      start=(k == 0), stop=(k == 7)), "TE")
            te_r = cnt["TE"]

            rb0 = 32 * (iw % 3)

            @block.vector
            def _(vector):
                vector.wait_ge(TE, te_r)
                inc(vector.tensor_copy(out=rbf[rb0:rb0 + 32, :], in_=p_sm[0:32, 0:256]), "VE")
            free_psum("p_sm", "VE")
            v_r = cnt["VE"]

            for m in range(8):
                slot = "p_mm" if m % 2 == 0 else "p_mm2"
                pb = p_mm if slot == "p_mm" else p_mm2
                co = 0
                key = slot

                @block.tensor
                def _(tensor, m=m, pb=pb, co=co, key=key):
                    if m == 0:
                        tensor.wait_ge(VE, v_r)
                    wait_psum(tensor, key)
                    for k in range(8):
                        inc(tensor.matmul(out=pb[:, co:co + 256],
                                          lhsT=w_deq[:, iw, k, m * 128:(m + 1) * 128],
                                          rhs=rhs_bf[:, k, :],
                                          start=(k == 0), stop=False), "TE")
                    inc(tensor.matmul(out=pb[:, co:co + 256],
                                      lhsT=lb_slice(iw, m * 128, (m + 1) * 128),
                                      rhs=rbf[rb0:rb0 + 32, :],
                                      start=False, stop=True), "TE")
                te_m = cnt["TE"]

                @block.scalar
                def _(scalar, m=m, pb=pb, co=co):
                    scalar.wait_ge(TE, te_m)
                    inc(scalar.activation(out=yv[:, m, :], in_=pb[:, co:co + 256],
                                          func=AF.Identity, bias=cst[:, 112 + iw * 8 + m:113 + iw * 8 + m]), "SE")
                free_psum(key, "SE")
            return cnt["SE"]

        for b in range(B):
            for c in range(NC1):
                ci = ("ffn", b * NC1 + c)
                prev = ("ffn", b * NC1 + c - 1) if (b * NC1 + c - 1) >= 0 else ("ffn", -1)

                @block.sync
                def _(sync, b=b, c=c):
                    chunk_gate(sync, hist[prev])
                    inc(sync.dma_start(
                        out=x_tm[:].rearrange("p (a d) -> p a d", a=2),
                        in_=x_in[b, T1 * c:T1 * (c + 1), :].rearrange("(a p) d -> p a d", p=128)),
                        "LD", 16)
                ldx = cnt["LD"]

                v_xn = emit_ln(ldx, 2)

                te_tp = {}
                for a in range(2):
                    dst = p_tp if a == 0 else p_tp2
                    name = "p_tp" if a == 0 else "p_tp2"
                    srcs = [xn_tm[:, a * 1024 + d * 128:a * 1024 + (d + 1) * 128] for d in range(8)]
                    te_tp[a] = emit_tp(srcs, dst, name, v_xn)

                @block.vector
                def _(vector):
                    vector.wait_ge(TE, te_tp[1])
                    inc(vector.tensor_copy(out=xnT_f[:, :, 0:128], in_=p_tp[:].rearrange("p (d t) -> p d t", d=8)), "VE")
                    inc(vector.tensor_copy(out=xnT_f[:, :, 128:256], in_=p_tp2[:].rearrange("p (d t) -> p d t", d=8)), "VE")

                @block.scalar
                def _(scalar):
                    scalar.wait_ge(TE, te_tp[1])
                    inc(scalar.activation(out=h0[:, :, 0:128], in_=p_tp[:].rearrange("p (d t) -> p d t", d=8),
                                          func=AF.Copy), "SE")
                    inc(scalar.activation(out=h0[:, :, 128:256], in_=p_tp2[:].rearrange("p (d t) -> p d t", d=8),
                                          func=AF.Copy), "SE")
                free_psum("p_tp", "SE", "VE")
                free_psum("p_tp2", "SE", "VE")
                v_rdy = cnt["VE"]

                for blk in range(3):
                    hsrc = h0 if blk % 2 == 0 else h1
                    hdst = h1 if blk % 2 == 0 else h0
                    i0, i1 = 2 * blk, 2 * blk + 1

                    if blk == 0:
                        rhs0 = xnT_f
                        rsem, rv = "VE", v_rdy
                    else:
                        @block.scalar
                        def _(scalar, hsrc=hsrc):
                            scalar.wait_ge(VE, cnt["VE"])
                            inc(scalar.activation(out=hbf[:], in_=hsrc[:], func=AF.Copy), "SE")
                        rhs0 = hbf
                        rsem, rv = "SE", cnt["SE"]

                    s_y1 = ffn_half(i0, rhs0, rsem, rv)

                    @block.scalar
                    def _(scalar):
                        inc(scalar.activation(out=ybf[:], in_=yv[:], func=AF.Copy), "SE")
                        inc(scalar.activation(out=y2bf[:], in_=yv[:], func=AF.Square), "SE")
                    s_cast = cnt["SE"]

                    @block.tensor
                    def _(tensor):
                        tensor.wait_ge(SE, s_cast)
                        wait_psum(tensor, "p_sm")
                        for k in range(8):
                            inc(tensor.matmul(out=p_sm[0:1, 0:256], lhsT=onescol_bf[:],
                                              rhs=ybf[:, k, :], start=(k == 0), stop=(k == 7)), "TE")
                        for k in range(8):
                            inc(tensor.matmul(out=p_sm[0:1, 256:512], lhsT=onescol_bf[:],
                                              rhs=y2bf[:, k, :], start=(k == 0), stop=(k == 7)), "TE")
                    te_mu = cnt["TE"]

                    @block.vector
                    def _(vector):
                        vector.wait_ge(TE, te_mu)
                        inc(vector.tensor_scalar(out=mrs[:, 0:256], in0=p_sm[0:1, 0:256],
                                                 scalar1=R1024, scalar2=None, op0=ALU.mult), "VE")
                        inc(vector.tensor_scalar(out=mrs[:, 256:512], in0=p_sm[0:1, 256:512],
                                                 scalar1=R1024, scalar2=None, op0=ALU.mult), "VE")
                        vdrain(vector)
                        inc(vector.tensor_tensor(out=mrs[:, 512:768], in0=mrs[:, 0:256],
                                                 in1=mrs[:, 0:256], op=ALU.mult), "VE")
                        vdrain(vector)
                        inc(vector.tensor_tensor(out=mrs[:, 256:512], in0=mrs[:, 256:512],
                                                 in1=mrs[:, 512:768], op=ALU.subtract), "VE")
                        vdrain(vector)
                    free_psum("p_sm", "VE")
                    v_vv = cnt["VE"]

                    @block.scalar
                    def _(scalar):
                        scalar.wait_ge(VE, v_vv)
                        inc(scalar.activation(out=mrs[:, 256:512], in_=mrs[:, 256:512],
                                              func=AF.Sqrt, bias=cst[0:1, 464:465]), "SE")
                    s_rsd = cnt["SE"]

                    @block.vector
                    def _(vector):
                        vector.wait_ge(SE, s_rsd)
                        inc(vector.reciprocal(out=mrs[:, 256:512], in_=mrs[:, 256:512]), "VE")
                        vdrain(vector)
                        inc(vector.tensor_copy(out=mrs_bf[:, 0:512], in_=mrs[:, 0:512]), "VE")
                        vdrain(vector)
                    v_mbf = cnt["VE"]

                    @block.tensor
                    def _(tensor):
                        tensor.wait_ge(VE, v_mbf)
                        wait_psum(tensor, "p_ln")
                        inc(tensor.matmul(out=p_ln[:, 0:512], lhsT=ones1_bf[:],
                                          rhs=mrs_bf[:, 0:512], start=True, stop=True), "TE")
                    te_b = cnt["TE"]

                    @block.vector
                    def _(vector):
                        vector.wait_ge(TE, te_b)
                        inc(vector.tensor_tensor(
                            out=yv[:], in0=yv[:],
                            in1=p_ln[:, 0:256].rearrange("p (o t) -> p o t", o=1).broadcast_to([128, 8, 256]),
                            op=ALU.subtract), "VE")
                        inc(vector.tensor_tensor(
                            out=yv[:], in0=yv[:],
                            in1=p_ln[:, 256:512].rearrange("p (o t) -> p o t", o=1).broadcast_to([128, 8, 256]),
                            op=ALU.mult), "VE")
                    free_psum("p_ln", "VE")
                    v_lnd = cnt["VE"]

                    @block.scalar
                    def _(scalar, blk=blk):
                        scalar.wait_ge(VE, v_lnd)
                        for m in range(8):
                            inc(scalar.activation(out=ybf[:, m, :], in_=yv[:, m, :],
                                                  func=AF.Gelu,
                                                  scale=cst[:, 160 + blk * 8 + m:161 + blk * 8 + m],
                                                  bias=cst[:, 184 + blk * 8 + m:185 + blk * 8 + m]), "SE")
                    s_gelu = cnt["SE"]

                    s_y2 = ffn_half(i1, ybf, "SE", s_gelu)

                    @block.vector
                    def _(vector, hsrc=hsrc, hdst=hdst):
                        vector.wait_ge(SE, s_y2)
                        inc(vector.tensor_tensor(out=hdst[:], in0=yv[:], in1=hsrc[:], op=ALU.add), "VE")

                @block.scalar
                def _(scalar):
                    scalar.wait_ge(VE, cnt["VE"])
                    inc(scalar.activation(out=gT_bf[:], in_=h1[:], func=AF.Silu), "SE")
                s_g = cnt["SE"]

                for a in range(2):
                    dst = p_tp if a == 0 else p_tp2
                    name = "p_tp" if a == 0 else "p_tp2"

                    @block.tensor
                    def _(tensor, a=a, dst=dst, name=name):
                        tensor.wait_ge(SE, s_g)
                        wait_psum(tensor, name)
                        for d in range(8):
                            inc(tensor.transpose(out=dst[:, d * 128:(d + 1) * 128],
                                                 in_=gT_bf[:, d, a * 128:(a + 1) * 128],
                                                 identity=id_bf[:]), "TE")
                    te_g = cnt["TE"]

                    @block.vector
                    def _(vector, a=a, dst=dst):
                        vector.wait_ge(TE, te_g)
                        inc(vector.tensor_copy(out=gtm[:, a * 1024:(a + 1) * 1024], in_=dst[:]), "VE")
                    free_psum(name, "VE")
                v_g = cnt["VE"]

                @block.gpsimd
                def _(gpsimd, b=b, c=c):
                    gpsimd.wait_ge(VE, v_g)
                    inc(gpsimd.dma_start(out=g_scr[b, T1 * c:T1 * c + 128, :],
                                         in_=gtm[:, 0:1024]), "ST", 16)
                    inc(gpsimd.dma_start(out=g_scr[b, T1 * c + 128:T1 * c + 256, :],
                                         in_=gtm[:, 1024:2048]), "ST", 16)

                hist[ci] = dict(cnt)

        @block.sync
        def _(sync):
            chunk_gate(sync, dict(cnt))
        s1.close()

        s2a = ExitStack()
        E2 = s2a.enter_context
        winz = E2(nc.sbuf_tensor("winz", [128, 8, 2048], BF16))
        wbufA = E2(nc.sbuf_tensor("wbufA", [128, 2048], F32))
        sz_bf = E2(nc.sbuf_tensor("sz_bf", [128, 16, 128], BF16))

        for k in range(8):
            @block.sync
            def _(sync, k=k):
                if k > 0:
                    sync.wait_ge(VE, cnt["VE"])
                inc(sync.dma_start(out=wbufA[:], in_=w_in[128 * k:128 * (k + 1), DI:2 * DI]), "LD", 16)
            ldw = cnt["LD"]

            @block.vector
            def _(vector, k=k):
                vector.wait_ge(LD, ldw)
                inc(vector.tensor_copy(out=winz[:, k, :], in_=wbufA[:]), "VE")

        hist[("2a", -1)] = dict(cnt)
        for b in range(B):
            for c in range(NC2):
                ci = ("2a", b * NC2 + c)
                prev = ("2a", b * NC2 + c - 1)

                @block.sync
                def _(sync, b=b, c=c):
                    chunk_gate(sync, hist[prev])
                    inc(sync.dma_start(
                        out=x_tm[:, 0:1024],
                        in_=x_in[b, T2 * c:T2 * (c + 1), :]), "LD", 16)
                ldx = cnt["LD"]

                v_xn = emit_ln(ldx, 1)
                srcs = [xn_tm[:, d * 128:(d + 1) * 128] for d in range(8)]
                te_t = emit_tp(srcs, p_tp, "p_tp", v_xn)

                @block.vector
                def _(vector):
                    vector.wait_ge(TE, te_t)
                    inc(vector.tensor_copy(out=xnT_m[:], in_=p_tp[:].rearrange("p (d t) -> p d t", d=8)), "VE")
                free_psum("p_tp", "VE")
                v_x = cnt["VE"]

                for g in range(4):
                    slot = "p_mm" if g % 2 == 0 else "p_mm2"
                    pb = p_mm if g % 2 == 0 else p_mm2

                    @block.tensor
                    def _(tensor, g=g, pb=pb, slot=slot):
                        if g == 0:
                            tensor.wait_ge(VE, v_x)
                        wait_psum(tensor, slot, slot + "0", slot + "1")
                        for ml in range(4):
                            m = 4 * g + ml
                            for k in range(8):
                                inc(tensor.matmul(out=pb[:, ml * 128:(ml + 1) * 128],
                                                  lhsT=winz[:, k, m * 128:(m + 1) * 128],
                                                  rhs=xnT_m[:, k, :],
                                                  start=(k == 0), stop=(k == 7)), "TE")
                    te_z = cnt["TE"]

                    @block.scalar
                    def _(scalar, g=g, pb=pb):
                        scalar.wait_ge(TE, te_z)
                        inc(scalar.activation(out=sz_bf[:, 4 * g:4 * (g + 1), :],
                                              in_=pb[:].rearrange("p (a t) -> p a t", a=4),
                                              func=AF.Silu), "SE")
                    free_psum(slot, "SE")
                s_sz = cnt["SE"]

                @block.gpsimd
                def _(gpsimd, b=b, c=c):
                    gpsimd.wait_ge(SE, s_sz)
                    inc(gpsimd.dma_start(out=sz_scr[b, c], in_=sz_bf[:].rearrange("p a t -> p (a t)")), "ST", 16)

                hist[ci] = dict(cnt)

        @block.sync
        def _(sync):
            chunk_gate(sync, dict(cnt))
        s2a.close()

        s2b = ExitStack()
        E3 = s2b.enter_context
        winx = E3(nc.sbuf_tensor("winx", [128, 8, 2048], BF16))
        wxp_bf = E3(nc.sbuf_tensor("wxp_bf", [128, 16, 96], BF16))
        wdt_bf = E3(nc.sbuf_tensor("wdt_bf", [64, 2048], BF16))
        wout_bf = E3(nc.sbuf_tensor("wout_bf", [128, 16, 1024], BF16))
        wbuf = x_tm
        xpart = E3(nc.sbuf_tensor("xpart", [128, 16, 131], F32))
        xconv = E3(nc.sbuf_tensor("xconv", [128, 16, 128], F32))
        xcv_bf = E3(nc.sbuf_tensor("xcv_bf", [128, 16, 128], BF16))
        dtl_bf = E3(nc.sbuf_tensor("dtl_bf", [64, 128], BF16))
        bc_sb = E3(nc.sbuf_tensor("bc_sb", [32, 128], BF16))
        Brep = E3(nc.sbuf_tensor("Brep", [128, 2048], BF16))
        Crep = E3(nc.sbuf_tensor("Crep", [128, 2048], BF16))
        dt_sb = E3(nc.sbuf_tensor("dt_sb", [128, 16, 128], F32))
        dtx = E3(nc.sbuf_tensor("dtx", [128, 16, 128], BF16))
        a_sc = E3(nc.sbuf_tensor("a_sc", [128, 2064], F32))
        b_sc = E3(nc.sbuf_tensor("b_sc", [128, 2064], F32))
        z_sc = E3(nc.sbuf_tensor("z_sc", [128, 2064], BF16))
        zc_sc = E3(nc.sbuf_tensor("zc_sc", [128, 2048], BF16))
        ytmp = E3(nc.sbuf_tensor("ytmp", [128, 1024], F32))
        ytb = E3(nc.sbuf_tensor("ytb", [128, 128], BF16))
        ybf_m = E3(nc.sbuf_tensor("ybf_m", [128, 16, 128], BF16))
        carry = E3(nc.sbuf_tensor("carry", [128, 256], F32))
        szbuf = E3(nc.sbuf_tensor("szbuf", [128, 16, 128], BF16))
        gbuf = E3(nc.sbuf_tensor("gbuf", [128, 1024], BF16))
        sT_bf = E3(nc.sbuf_tensor("sT_bf", [128, 8, 128], BF16))

        for k in range(8):
            @block.sync
            def _(sync, k=k):
                sync.wait_ge(VE, cnt["VE"])
                inc(sync.dma_start(out=wbuf[:], in_=w_in[128 * k:128 * (k + 1), 0:DI]), "LD", 16)
            ldw = cnt["LD"]

            @block.vector
            def _(vector, k=k):
                vector.wait_ge(LD, ldw)
                inc(vector.tensor_copy(out=winx[:, k, :], in_=wbuf[:]), "VE")

        @block.sync
        def _(sync):
            sync.wait_ge(VE, cnt["VE"])
            inc(sync.dma_start(out=wbuf[:, 0:1536].rearrange("p (t o) -> p t o", t=16),
                               in_=wxp_in.rearrange("(t p) o -> p t o", p=128)), "LD", 16)
        ldw = cnt["LD"]

        @block.vector
        def _(vector):
            vector.wait_ge(LD, ldw)
            inc(vector.tensor_copy(out=wxp_bf[:], in_=wbuf[:, 0:1536].rearrange("p (t o) -> p t o", t=16)), "VE")

        @block.sync
        def _(sync):
            sync.wait_ge(VE, cnt["VE"])
            inc(sync.dma_start(out=wbuf[0:64, 0:2048], in_=wdt_in), "LD", 16)
        ldw = cnt["LD"]

        @block.vector
        def _(vector):
            vector.wait_ge(LD, ldw)
            inc(vector.tensor_copy(out=wdt_bf[:], in_=wbuf[0:64, 0:2048]), "VE")

        for t2 in range(8):
            @block.sync
            def _(sync, t2=t2):
                sync.wait_ge(VE, cnt["VE"])
                inc(sync.dma_start(
                    out=wbuf[:].rearrange("p (a o) -> p a o", a=2),
                    in_=wout_in[256 * t2:256 * (t2 + 1), :].rearrange("(a p) o -> p a o", p=128)), "LD", 16)
            ldw = cnt["LD"]

            @block.vector
            def _(vector, t2=t2):
                vector.wait_ge(LD, ldw)
                inc(vector.tensor_copy(out=wout_bf[:, 2 * t2:2 * t2 + 2, :],
                                       in_=wbuf[:].rearrange("p (a o) -> p a o", a=2)), "VE")

        hist[("2b", -1)] = dict(cnt)
        for b in range(B):
            for c in range(NC2):
                ci = ("2b", b * NC2 + c)
                prev = ("2b", b * NC2 + c - 1)

                @block.sync
                def _(sync, b=b, c=c):
                    chunk_gate(sync, hist[prev])
                    inc(sync.dma_start(out=x_tm[:, 0:1024], in_=x_in[b, T2 * c:T2 * (c + 1), :]), "LD", 16)
                    inc(sync.dma_start(out=szbuf[:], in_=sz_scr[b, c]), "LD", 16)
                    inc(sync.dma_start(out=gbuf[:], in_=g_scr[b, T2 * c:T2 * (c + 1), :]), "LD", 16)
                ldx = cnt["LD"]

                v_xn = emit_ln(ldx, 1)
                srcs = [xn_tm[:, d * 128:(d + 1) * 128] for d in range(8)]
                te_t = emit_tp(srcs, p_tp, "p_tp", v_xn)

                @block.vector
                def _(vector, b=b, c=c):
                    vector.wait_ge(TE, te_t)
                    inc(vector.tensor_copy(out=xnT_m[:], in_=p_tp[:].rearrange("p (d t) -> p d t", d=8)), "VE")
                    if c == 0:
                        inc(vector.memset(xpart[:, :, 0:3], 0.0), "VE")
                        inc(vector.memset(carry[:], 0.0), "VE")
                free_psum("p_tp", "VE")
                v_x = cnt["VE"]

                for g in range(4):
                    slot = "p_mm" if g % 2 == 0 else "p_mm2"
                    pb = p_mm if g % 2 == 0 else p_mm2

                    @block.tensor
                    def _(tensor, g=g, pb=pb, slot=slot):
                        if g == 0:
                            tensor.wait_ge(VE, v_x)
                        wait_psum(tensor, slot, slot + "0", slot + "1")
                        for ml in range(4):
                            m = 4 * g + ml
                            for k in range(8):
                                inc(tensor.matmul(out=pb[:, ml * 128:(ml + 1) * 128],
                                                  lhsT=winx[:, k, m * 128:(m + 1) * 128],
                                                  rhs=xnT_m[:, k, :],
                                                  start=(k == 0), stop=(k == 7)), "TE")
                    te_xz = cnt["TE"]

                    @block.vector
                    def _(vector, g=g, pb=pb):
                        vector.wait_ge(TE, te_xz)
                        inc(vector.tensor_copy(out=xpart[:, 4 * g:4 * (g + 1), 3:131],
                                               in_=pb[:].rearrange("p (a t) -> p a t", a=4)), "VE")
                    free_psum(slot, "VE")

                @block.vector
                def _(vector):
                    for t in range(16):
                        inc(vector.tensor_scalar(out=xconv[:, t, :], in0=xpart[:, t, 0:128],
                                                 scalar1=cst[:, 4 * t:4 * t + 1], scalar2=None,
                                                 op0=ALU.mult), "VE")
                        for kk in range(1, 4):
                            vdrain(vector)
                            inc(vector.scalar_tensor_tensor(out=xconv[:, t, :], in0=xpart[:, t, kk:kk + 128],
                                                            scalar=cst[:, 4 * t + kk:4 * t + kk + 1],
                                                            in1=xconv[:, t, :],
                                                            op0=ALU.mult, op1=ALU.add), "VE")
                    inc(vector.tensor_copy(out=xpart[:, :, 0:3], in_=xpart[:, :, 128:131]), "VE")
                v_cv = cnt["VE"]

                @block.scalar
                def _(scalar):
                    scalar.wait_ge(VE, v_cv)
                    for t in range(16):
                        inc(scalar.activation(out=xconv[:, t, :], in_=xconv[:, t, :],
                                              func=AF.Silu, bias=cst[:, 64 + t:65 + t]), "SE")
                    inc(scalar.activation(out=xcv_bf[:], in_=xconv[:], func=AF.Copy), "SE")
                s_cv = cnt["SE"]

                @block.tensor
                def _(tensor):
                    tensor.wait_ge(SE, s_cv)
                    wait_psum(tensor, "p_sm")
                    for k in range(16):
                        inc(tensor.matmul(out=p_sm[0:96, 0:128], lhsT=wxp_bf[:, k, :],
                                          rhs=xcv_bf[:, k, :], start=(k == 0), stop=(k == 15)), "TE")
                te_pj = cnt["TE"]

                @block.vector
                def _(vector):
                    vector.wait_ge(TE, te_pj)
                    inc(vector.tensor_copy(out=dtl_bf[:], in_=p_sm[0:64, 0:128]), "VE")
                    inc(vector.tensor_copy(out=bc_sb[:], in_=p_sm[64:96, 0:128]), "VE")
                free_psum("p_sm", "VE")
                v_pj = cnt["VE"]

                @block.gpsimd
                def _(gpsimd):
                    gpsimd.wait_ge(VE, v_pj)
                    inc(gpsimd.dma_start(out=bc_scr, in_=bc_sb[:]), "ST", 16)
                st_bc = cnt["ST"]

                @block.sync
                def _(sync):
                    sync.wait_ge(ST, st_bc)
                    inc(sync.dma_start(out=gtm[0:1, 0:2048],
                                       in_=bc_scr[0:16, :].rearrange("s t -> (s t)")), "LD", 16)
                    inc(sync.dma_start(out=xn_tm[0:1, 0:2048],
                                       in_=bc_scr[16:32, :].rearrange("s t -> (s t)")), "LD", 16)
                ld_bc = cnt["LD"]

                for piece in range(4):
                    slot = "p_rep" if piece % 2 == 0 else "p_ln"
                    pb = p_rep if piece % 2 == 0 else p_ln

                    @block.tensor
                    def _(tensor, piece=piece, pb=pb, slot=slot):
                        tensor.wait_ge(LD, ld_bc)
                        wait_psum(tensor, slot)
                        inc(tensor.matmul(out=pb[:, 0:512], lhsT=ones1_bf[:],
                                          rhs=gtm[0:1, 512 * piece:512 * (piece + 1)],
                                          start=True, stop=True), "TE")
                    te_r = cnt["TE"]

                    @block.vector
                    def _(vector, piece=piece, pb=pb):
                        vector.wait_ge(TE, te_r)
                        inc(vector.tensor_copy(out=Brep[:, 512 * piece:512 * (piece + 1)],
                                               in_=pb[:, 0:512]), "VE")
                    free_psum(slot, "VE")
                for piece in range(4):
                    slot = "p_rep" if piece % 2 == 0 else "p_ln"
                    pb = p_rep if piece % 2 == 0 else p_ln

                    @block.tensor
                    def _(tensor, piece=piece, pb=pb, slot=slot):
                        wait_psum(tensor, slot)
                        inc(tensor.matmul(out=pb[:, 0:512], lhsT=ones1_bf[:],
                                          rhs=xn_tm[0:1, 512 * piece:512 * (piece + 1)],
                                          start=True, stop=True), "TE")
                    te_r = cnt["TE"]

                    @block.vector
                    def _(vector, piece=piece, pb=pb):
                        vector.wait_ge(TE, te_r)
                        inc(vector.tensor_copy(out=Crep[:, 512 * piece:512 * (piece + 1)],
                                               in_=pb[:, 0:512]), "VE")
                    free_psum(slot, "VE")

                for g in range(4):
                    slot = "p_mm" if g % 2 == 0 else "p_mm2"
                    pb = p_mm if g % 2 == 0 else p_mm2

                    @block.tensor
                    def _(tensor, g=g, pb=pb, slot=slot):
                        wait_psum(tensor, slot, slot + "0", slot + "1")
                        for ml in range(4):
                            m = 4 * g + ml
                            inc(tensor.matmul(out=pb[:, ml * 128:(ml + 1) * 128],
                                              lhsT=wdt_bf[:, m * 128:(m + 1) * 128],
                                              rhs=dtl_bf[:], start=True, stop=True), "TE")
                    te_d = cnt["TE"]

                    @block.scalar
                    def _(scalar, g=g, pb=pb):
                        scalar.wait_ge(TE, te_d)
                        for ml in range(4):
                            m = 4 * g + ml
                            inc(scalar.activation(out=dt_sb[:, m, :], in_=pb[:, ml * 128:(ml + 1) * 128],
                                                  func=AF.Exp, bias=cst[:, 80 + m:81 + m]), "SE")
                    free_psum(slot, "SE")

                @block.scalar
                def _(scalar):
                    sdrain(scalar)
                    inc(scalar.activation(out=dt_sb[:].rearrange("p t e -> p (t e)"),
                                          in_=dt_sb[:].rearrange("p t e -> p (t e)"),
                                          func=AF.Ln, bias=1.0), "SE")
                s_dt = cnt["SE"]

                @block.vector
                def _(vector):
                    vector.wait_ge(SE, s_dt)
                    inc(vector.tensor_tensor(out=dtx[:].rearrange("p t e -> p (t e)"),
                                             in0=dt_sb[:].rearrange("p t e -> p (t e)"),
                                             in1=xconv[:].rearrange("p t e -> p (t e)"),
                                             op=ALU.mult), "VE")
                v_dtx = cnt["VE"]

                a3 = a_sc[:].rearrange("p (s e) -> p s e", s=16)
                b3 = b_sc[:].rearrange("p (s e) -> p s e", s=16)
                z3 = z_sc[:].rearrange("p (s e) -> p s e", s=16)
                zc3 = zc_sc[:].rearrange("p (s e) -> p s e", s=16)
                v_gate = [cnt["VE"]]
                for til in range(16):
                    @block.scalar
                    def _(scalar, til=til, v_g=v_gate[0]):
                        scalar.wait_ge(VE, v_g)
                        for s in range(16):
                            inc(scalar.activation(out=a3[:, s, 1:129], in_=dt_sb[:, til, :],
                                                  func=AF.Exp,
                                                  scale=acol[:, til * 16 + s:til * 16 + s + 1]), "SE")
                    s_a = cnt["SE"]

                    @block.vector
                    def _(vector, til=til):
                        vector.wait_ge(SE, s_a)
                        inc(vector.memset(a3[:, :, 0:1], 0.0), "VE")
                        inc(vector.tensor_tensor(
                            out=b3[:, :, 1:129],
                            in0=dtx[:, til, :].rearrange("p (o e) -> p o e", o=1).broadcast_to([128, 16, 128]),
                            in1=Brep[:].rearrange("p (s e) -> p s e", s=16), op=ALU.mult), "VE")
                        inc(vector.tensor_copy(out=b3[:, :, 0], in_=carry[:, til * 16:(til + 1) * 16]), "VE")
                        vdrain(vector)
                        inc(vector.tensor_tensor_scan(out=z_sc[:], data0=a_sc[:], data1=b_sc[:],
                                                      initial=0.0, op0=ALU.mult, op1=ALU.add), "VE")
                        v_gate[0] = cnt["VE"]
                        vdrain(vector)
                        inc(vector.tensor_copy(out=carry[:, til * 16:(til + 1) * 16], in_=z3[:, :, 128]), "VE")
                        inc(vector.tensor_tensor(out=zc3[:], in0=z3[:, :, 1:129],
                                                 in1=Crep[:].rearrange("p (s e) -> p s e", s=16),
                                                 op=ALU.mult), "VE")
                        inc(vector.tensor_tensor(out=ytmp[:, 0:1024], in0=zc_sc[:, 0:1024],
                                                 in1=zc_sc[:, 1024:2048], op=ALU.add), "VE")
                        vdrain(vector)
                        inc(vector.tensor_tensor(out=ytmp[:, 0:512], in0=ytmp[:, 0:512],
                                                 in1=ytmp[:, 512:1024], op=ALU.add), "VE")
                        vdrain(vector)
                        inc(vector.tensor_tensor(out=ytmp[:, 0:256], in0=ytmp[:, 0:256],
                                                 in1=ytmp[:, 256:512], op=ALU.add), "VE")
                        vdrain(vector)
                        inc(vector.tensor_tensor(out=ytmp[:, 0:128], in0=ytmp[:, 0:128],
                                                 in1=ytmp[:, 128:256], op=ALU.add), "VE")
                        vdrain(vector)
                        inc(vector.scalar_tensor_tensor(out=ytb[:], in0=xconv[:, til, :],
                                                        scalar=cst[:, 96 + til:97 + til],
                                                        in1=ytmp[:, 0:128],
                                                        op0=ALU.mult, op1=ALU.add), "VE")
                        vdrain(vector)
                        inc(vector.tensor_tensor(out=ybf_m[:, til, :], in0=ytb[:],
                                                 in1=szbuf[:, til, :], op=ALU.mult), "VE")

                v_scan = cnt["VE"]

                for g in range(2):
                    slot = "p_mm" if g % 2 == 0 else "p_mm2"
                    pb = p_mm if g % 2 == 0 else p_mm2

                    @block.tensor
                    def _(tensor, g=g, pb=pb, slot=slot):
                        tensor.wait_ge(VE, v_scan)
                        wait_psum(tensor, slot, slot + "0", slot + "1")
                        for ml in range(4):
                            m = 4 * g + ml
                            for k in range(16):
                                inc(tensor.matmul(out=pb[:, ml * 128:(ml + 1) * 128],
                                                  lhsT=wout_bf[:, k, m * 128:(m + 1) * 128],
                                                  rhs=ybf_m[:, k, :],
                                                  start=(k == 0), stop=(k == 15)), "TE")
                    te_o = cnt["TE"]

                    @block.scalar
                    def _(scalar, g=g, pb=pb):
                        scalar.wait_ge(TE, te_o)
                        inc(scalar.activation(out=sT_bf[:, 4 * g:4 * (g + 1), :],
                                              in_=pb[:].rearrange("p (a t) -> p a t", a=4),
                                              func=AF.Copy), "SE")
                    free_psum(slot, "SE")
                s_o = cnt["SE"]

                @block.tensor
                def _(tensor):
                    tensor.wait_ge(SE, s_o)
                    wait_psum(tensor, "p_tp2")
                    for d in range(8):
                        inc(tensor.transpose(out=p_tp2[:, d * 128:(d + 1) * 128],
                                             in_=sT_bf[:, d, :], identity=id_bf[:]), "TE")
                te_s = cnt["TE"]

                @block.vector
                def _(vector):
                    vector.wait_ge(TE, te_s)
                    inc(vector.tensor_tensor(out=cmb[:], in0=p_tp2[:], in1=gbuf[:], op=ALU.mult), "VE")
                    inc(vector.tensor_tensor(out=out_tm[:], in0=cmb[:], in1=x_tm[:, 0:1024], op=ALU.add), "VE")
                free_psum("p_tp2", "VE")
                v_out = cnt["VE"]

                @block.gpsimd
                def _(gpsimd, b=b, c=c):
                    gpsimd.wait_ge(VE, v_out)
                    inc(gpsimd.dma_start(out=out_hbm[b, T2 * c:T2 * (c + 1), :], in_=out_tm[:]), "ST", 16)

                hist[ci] = dict(cnt)

        @block.sync
        def _(sync):
            sync.wait_ge(TE, cnt["TE"])
            sync.wait_ge(VE, cnt["VE"])
            sync.wait_ge(SE, cnt["SE"])
            sync.wait_ge(ST, cnt["ST"])
            sync.wait_ge(LD, cnt["LD"])
        s2b.close()
    return nc


_NC_CACHE = None


def kernel(L0_attn_hidden_states, L0_attn_W_in, L0_attn_conv_w, L0_attn_conv_b,
           L0_attn_W_x, L0_attn_W_dt, L0_attn_b_dt, L0_attn_A_log, L0_attn_D,
           L0_attn_W_out, L0_ffn_qweights, L0_ffn_scales, L0_ffn_biases,
           L0_ffn_lora_a, L0_ffn_lora_b, L0_ffn_ln_gamma, L0_ffn_ln_beta):
    global _NC_CACHE
    x = np.ascontiguousarray(np.asarray(L0_attn_hidden_states, dtype=np.float32))
    f = lambda a: np.ascontiguousarray(np.asarray(a, dtype=np.float32))
    w_in = f(L0_attn_W_in)
    conv_w = f(L0_attn_conv_w)
    conv_b = f(L0_attn_conv_b)
    wxp = f(L0_attn_W_x)
    wdt = f(L0_attn_W_dt)
    b_dt = f(L0_attn_b_dt)
    alog = f(L0_attn_A_log)
    dD = f(L0_attn_D)
    wout = f(L0_attn_W_out)
    q = np.ascontiguousarray(np.asarray(L0_ffn_qweights, dtype=np.int32))
    sc = f(L0_ffn_scales)
    bias = f(L0_ffn_biases)
    la = f(L0_ffn_lora_a)
    lb = f(L0_ffn_lora_b)
    gam = f(L0_ffn_ln_gamma)
    bet = f(L0_ffn_ln_beta)

    qT = np.ascontiguousarray(q.transpose(0, 2, 1))
    scT = np.ascontiguousarray(sc.transpose(0, 2, 1))
    laT = np.ascontiguousarray(la.transpose(0, 2, 1))
    lbT = np.ascontiguousarray(lb.transpose(0, 2, 1))
    convw_c = np.ascontiguousarray(conv_w.T.reshape(16, 128, 4).transpose(1, 0, 2).reshape(128, 64))
    convb_c = np.ascontiguousarray(conv_b.reshape(16, 128).T)
    bdt_c = np.ascontiguousarray(b_dt.reshape(16, 128).T)
    dD_c = np.ascontiguousarray(dD.reshape(16, 128).T)
    alog_c = np.ascontiguousarray(alog.reshape(16, 128, 16).transpose(1, 0, 2).reshape(128, 256))
    bias_c = np.ascontiguousarray(bias.reshape(6, 8, 128).transpose(2, 0, 1).reshape(128, 48))
    gam_c = np.ascontiguousarray(gam.reshape(3, 8, 128).transpose(2, 0, 1).reshape(128, 24))
    bet_c = np.ascontiguousarray(bet.reshape(3, 8, 128).transpose(2, 0, 1).reshape(128, 24))
    id_c = np.eye(128, dtype=np.float32)
    ones16 = np.ones((16, 128), dtype=np.float32)
    ones1 = np.ones((1, 128), dtype=np.float32)
    onescol = np.ones((128, 1), dtype=np.float32)
    e64 = np.zeros((64, 8, 128), dtype=np.float32)
    for t in range(8):
        for p in range(128):
            e64[8 * t + p // 16, t, p] = 1.0
    e64 = np.ascontiguousarray(np.concatenate([e64.reshape(64, 1024)] * 2, axis=0))

    if _NC_CACHE is None:
        _NC_CACHE = build_nc()
    nc = _NC_CACHE

    shared = dict(w_in=w_in, wxp_in=wxp, wdt_in=wdt, wout_in=wout,
                  convw_in=convw_c, convb_in=convb_c, bdt_in=bdt_c,
                  alog_in=alog_c, dD_in=dD_c, qT_in=qT, scT_in=scT,
                  bias_in=bias_c, laT_in=laT, lbT_in=lbT, gam_in=gam_c,
                  bet_in=bet_c, id_in=id_c, ones16_in=ones16, ones1_in=ones1,
                  onescol_in=onescol, e64_in=e64)
    in_maps = []
    for core in range(8):
        m = dict(shared)
        m["x_in"] = x[2 * core:2 * core + 2]
        in_maps.append(m)

    res = run_bass_kernel_spmd(nc, in_maps, core_ids=list(range(8)))
    outs = [res.results[i]["out"] for i in range(8)]
    full = np.ascontiguousarray(np.concatenate(outs, axis=0)).astype(np.float32)
    try:
        np.save("/tmp/kout.npy", full)
        np.save("/tmp/kg.npy", np.asarray(res.results[0]["g_scr"], dtype=np.float32))
        np.save("/tmp/ksz.npy", np.asarray(res.results[0]["sz_scr"], dtype=np.float32))
    except Exception as e:
        print("dump err", e)
    return full
